# version 01
# baseline (speedup 1.0000x reference)
import numpy as np
from contextlib import ExitStack

import concourse.bass as bass
import concourse.mybir as mybir
from concourse.bass_utils import run_bass_kernel_spmd

F32 = mybir.dt.float32
BF16 = mybir.dt.bfloat16
AF = mybir.ActivationFunctionType
ALU = mybir.AluOpType
AX = mybir.AxisListType

B, T, D = 4, 8192, 1024
TL = T // 2
NC8 = D // 128
PAIRS = [[0, 1], [2, 3], [4, 5], [6, 7]]
LN_EPS = 1e-5
FP4_MIDS = [0.25, 0.75, 1.25, 1.75, 2.5, 3.5, 5.0]


class Sem:
    def __init__(self, ctx, nc, name):
        self.h = ctx.enter_context(nc.semaphore(name))
        self.n = 0


class K:
    def __init__(self):
        self.nc = bass.Bass("TRN2", target_bir_lowering=False, debug=False,
                            detect_race_conditions=False)
        self.ctx = ExitStack()
        self.block = None
        self._sems = {}
        self._uid = 0
        self.phases = []

    def uid(self, p="t"):
        self._uid += 1
        return f"{p}{self._uid}"

    def sem(self, name):
        if name not in self._sems:
            self._sems[name] = Sem(self.ctx, self.nc, name)
        return self._sems[name]

    def sb(self, name, shape, dtype=F32):
        stack = self._scopes[-1] if getattr(self, "_scopes", None) else self.ctx
        return stack.enter_context(self.nc.sbuf_tensor(name, shape, dtype))

    def push_scope(self):
        if not hasattr(self, "_scopes"):
            self._scopes = []
        st = ExitStack()
        self._scopes.append(st)
        return st

    def pop_scope(self):
        self._scopes.pop().close()

    def psum(self, name, shape, dtype=F32):
        return self.ctx.enter_context(self.nc.psum_tensor(name, shape, dtype))

    def dram(self, name, shape, dtype=F32, **kw):
        return self.nc.dram_tensor(name, shape, dtype, **kw)

    def c(self, eng, wait=True):
        def deco(fn):
            self.phases.append(("c", eng, fn, wait))
            return fn
        return deco

    def d(self, eng, wait=True):
        def deco(fn):
            self.phases.append(("d", eng, fn, wait))
            return fn
        return deco

    def manual(self, fn):
        self.phases.append(("m", None, fn, True))
        return fn

    def snapshot(self):
        return ("snap", len(self.phases))

    def constcol(self, value):
        if not hasattr(self, "_consts"):
            self._consts = {}
        if value not in self._consts:
            self._consts[value] = self.ctx.enter_context(
                self.nc.sbuf_tensor(self.uid("CC"), [128, 1], F32))
        return self._consts[value]

    def finalize(self):
        ph = self.sem("PH")
        phg = self.sem("PHG")
        blk = self.ctx.enter_context(self.nc.Block())
        self.block = blk
        self._last_ndma = 0
        if getattr(self, "_consts", None):
            def cbody(v):
                ins = None
                for val, tile in self._consts.items():
                    ins = v.memset(tile[:], float(val))
                ins.then_inc(ph.h, 1)
            blk.vector(cbody)
            ph.n += 1
        for kind, eng, fn, wait in self.phases:
            if kind == "m":
                fn(self, blk, ph)
                continue
            total = ph.n
            totalg = phg.n
            is_sw = (kind == "d" and eng == "gpsimd")
            sec = getattr(blk, eng)

            def body(e, fn=fn, total=total, totalg=totalg, wait=wait,
                     kind=kind, is_sw=is_sw):
                if wait is True:
                    e.wait_ge(ph.h, total)
                    if totalg:
                        e.wait_ge(phg.h, totalg)
                elif wait is not False and wait is not None:
                    e.wait_ge(ph.h, wait)
                r = fn(e)
                if kind == "c":
                    assert r is not None, "compute phase must return last instr"
                    r.then_inc(ph.h, 1)
                else:
                    assert isinstance(r, (list, tuple)) and len(r) > 0
                    tgt = phg.h if is_sw else ph.h
                    for ins in r:
                        ins.then_inc(tgt, 16)
                    self._last_ndma = len(r)
                    if is_sw:
                        e.wait_ge(phg.h, totalg + 16 * len(r))
            sec(body)
            if kind == "c":
                ph.n += 1
            elif is_sw:
                phg.n += 16 * self._last_ndma
            else:
                ph.n += 16 * self._last_ndma
        return self.nc


def bcast_blocks(ap_small, ap_big):
    a, b = bass.broadcast_tensor_aps(ap_big, ap_small)
    return b



class Common:
    def __init__(self, k):
        self.k = k
        self.strm = k.sb("STRM", [128, NC8 * TL], F32)
        self.mu = k.sb("LNmu", [1, 512], F32)[0:1, :]
        self.rs = k.sb("LNrs", [1, 512], F32)[0:1, :]
        self.var = k.sb("LNvar", [1, 512], F32)[0:1, :]
        self.sq = k.sb("LNsq", [128, 4 * 512], F32)
        self.ones_col = k.sb("ONESC", [128, 1], F32)
        self.ones_colb = k.sb("ONESCB", [128, 1], BF16)
        self.ones = k.sb("ONESR", [128, 128], F32)
        self.ones_row = self.ones[0:1, :]
        self.one1 = k.sb("ONE1", [1, 1], F32)
        self.flag = k.sb("FLAG", [1, 1], F32)
        self.flagcol = k.sb("FLAGC", [128, 1], F32)
        self.flag_dram = k.dram("flag_in", [1, 1], F32, kind="ExternalInput").ap()
        self.pp = [k.psum(f"PP{i}", [128, 512], F32) for i in range(6)]
        self.pmu = k.psum("PMU", [1, 512], F32)
        self.psq = k.psum("PSQ", [1, 512], F32)
        for v in (LN_EPS, 1e-6, 1.0, 0.9999):
            k.constcol(v)

    def load_stream(self, src_ap):
        k = self.k

        @k.d("sync", wait=False)
        def _(s):
            return [s.dma_start(out=self.strm[:, c * TL:(c + 1) * TL],
                                in_=src_ap[c * 128:(c + 1) * 128, :])
                    for c in range(NC8)]

    def store_stream(self, dst_ap):
        k = self.k

        @k.d("sync")
        def _(s):
            return [s.dma_start(out=dst_ap[c * 128:(c + 1) * 128, :],
                                in_=self.strm[:, c * TL:(c + 1) * TL])
                    for c in range(NC8)]

    def load_consts(self):
        k = self.k

        @k.c("vector", wait=False)
        def _(v):
            v.memset(self.ones_col[:], 1.0)
            v.memset(self.ones_colb[:], 1.0)
            v.memset(self.ones[:, :], 1.0)
            return v.memset(self.one1[:1, :], 1.0)

        @k.d("sync", wait=False)
        def _(s):
            return [s.dma_start(out=self.flag[:1, :], in_=self.flag_dram)]

        @k.c("tensor")
        def _(t):
            return t.matmul(self.pp[0][:, 0:1], self.ones[0:1, :],
                            self.flag[0:1, 0:1], start=True, stop=True)

        @k.c("vector")
        def _(v):
            return v.tensor_copy(self.flagcol[:, :], self.pp[0][:, 0:1])

    def ln_stats_tile(self, t0, rms_fuse=False, rms_eps=1e-5, src=None,
                      src_stride=None, n=512):
        k = self.k
        st = self.strm if src is None else src
        stride = TL if src_stride is None else src_stride
        eps_c = k.constcol(LN_EPS)
        reps_c = k.constcol(rms_eps)
        for g in range(2):
            @k.c("scalar")
            def _(sc, g=g):
                ins = None
                for i in range(4):
                    c = g * 4 + i
                    ins = sc.activation(
                        out=self.sq[:, i * 512:i * 512 + n],
                        in_=st[:, c * stride + t0: c * stride + t0 + n],
                        func=AF.Square)
                return ins

            @k.c("tensor")
            def _(t, g=g):
                for i in range(4):
                    c = g * 4 + i
                    t.matmul(self.pmu[:1, 0:n], self.ones_col[:, 0:1],
                             st[:, c * stride + t0: c * stride + t0 + n],
                             start=(c == 0), stop=(c == NC8 - 1))
                ins = None
                for i in range(4):
                    ins = t.matmul(self.psq[:1, 0:n], self.ones_col[:, 0:1],
                                   self.sq[:, i * 512:i * 512 + n],
                                   start=(g == 0 and i == 0),
                                   stop=(g == 1 and i == 3))
                return ins

        @k.c("vector")
        def _(v):
            v.tensor_scalar(self.mu[:1, 0:n], self.pmu[:1, 0:n], 1.0 / D,
                            None, ALU.mult)
            v.tensor_scalar(self.var[:1, 0:n], self.psq[:1, 0:n], 1.0 / D,
                            None, ALU.mult)
            v.tensor_mul(self.rs[:1, 0:n], self.mu[:1, 0:n], self.mu[:1, 0:n])
            return v.tensor_sub(self.var[:1, 0:n], self.var[:1, 0:n],
                                self.rs[:1, 0:n])

        @k.c("scalar")
        def _(sc):
            return sc.activation(out=self.rs[:1, 0:n], in_=self.var[:1, 0:n],
                                 func=AF.Sqrt, bias=eps_c[0:1, :])

        @k.c("vector")
        def _(v):
            v.reciprocal(self.rs[:1, 0:n], self.rs[:1, 0:n])
            t = self.sq[0:1, 0:n]
            v.tensor_mul(t, self.rs[:1, 0:n], self.rs[:1, 0:n])
            v.tensor_scalar(t, t, 0.5, None, ALU.mult)
            v.scalar_tensor_tensor(t, self.var[:1, 0:n], LN_EPS, t,
                                   ALU.add, ALU.mult)
            v.tensor_scalar(t, t, -1.0, 1.5, ALU.mult, ALU.add)
            return v.tensor_mul(self.rs[:1, 0:n], self.rs[:1, 0:n], t)

        if rms_fuse:
            @k.c("vector")
            def _(v):
                v.tensor_mul(self.var[:1, 0:n], self.var[:1, 0:n],
                             self.rs[:1, 0:n])
                return v.tensor_mul(self.var[:1, 0:n], self.var[:1, 0:n],
                                    self.rs[:1, 0:n])

            @k.c("scalar")
            def _(sc):
                return sc.activation(out=self.var[:1, 0:n],
                                     in_=self.var[:1, 0:n],
                                     func=AF.Sqrt, bias=reps_c[0:1, :])

            @k.c("vector")
            def _(v):
                v.reciprocal(self.var[:1, 0:n], self.var[:1, 0:n])
                return v.tensor_mul(self.rs[:1, 0:n], self.rs[:1, 0:n],
                                    self.var[:1, 0:n])

    def ln_bcast(self, n=512):
        k = self.k

        @k.c("tensor")
        def _(t):
            t.matmul(self.pp[4][:, 0:n], self.ones[0:1, :],
                     self.mu[:1, 0:n], start=True, stop=True)
            return t.matmul(self.pp[5][:, 0:n], self.ones[0:1, :],
                            self.rs[:1, 0:n], start=True, stop=True)

    def vec_to_cols(self, row_ap, dst, n, pbank):
        k = self.k
        for c in range(n):
            @k.c("tensor")
            def _(t, c=c):
                return t.matmul(pbank[:, 0:1], row_ap[0:1, c * 128:(c + 1) * 128],
                                self.one1[0:1, :], start=True, stop=True)

            @k.c("vector")
            def _(v, c=c):
                return v.tensor_copy(dst[:, c:c + 1], pbank[:, 0:1])

    def row_bcast(self, row_ap, dst, W, pbank):
        k = self.k
        for j0 in range(0, W, 512):
            n = min(512, W - j0)

            @k.c("tensor")
            def _(t, j0=j0, n=n):
                return t.matmul(pbank[:, 0:n], self.ones[0:1, :],
                                row_ap[0:1, j0:j0 + n], start=True, stop=True)

            @k.c("vector")
            def _(v, j0=j0, n=n):
                return v.tensor_copy(dst[:, j0:j0 + n], pbank[:, 0:n])

    def ln_apply(self, t0, n, out, out_off=0, stride=None, chunks=None):
        k = self.k
        chunks = list(range(NC8)) if chunks is None else chunks
        stride = n if stride is None else stride

        @k.c("vector")
        def _(v):
            ins = None
            for i, c in enumerate(chunks):
                sl_in = self.strm[:, c * TL + t0: c * TL + t0 + n]
                tmp = self.sq[:, (i % 4) * 512:(i % 4) * 512 + n]
                v.tensor_sub(tmp, sl_in, self.pp[4][:, 0:n])
                ins = v.tensor_mul(
                    out[:, out_off + i * stride: out_off + i * stride + n],
                    tmp, self.pp[5][:, 0:n])
            return ins



def emit_fp4_quant(k, in_ap, out_ap, W, bufA, bufB, bufC, sraw, smax):
    G = W // 16
    in3 = in_ap.rearrange("p (g s) -> p g s", s=16)

    @k.c("vector")
    def _(v):
        v.reduce_max(sraw.rearrange("p (g s) -> p g s", s=1)[:, 0:G, :],
                     in3, axis=AX.X, apply_absolute_value=True)
        v.tensor_scalar(smax[:, 0:G], sraw[:, 0:G], 1.0 / 6.0, 1e-12,
                        ALU.mult, ALU.max)
        return v.reciprocal(sraw[:, 0:G], smax[:, 0:G])

    @k.c("vector")
    def _(v):
        inv_bc = bcast_blocks(
            sraw.rearrange("p (g s) -> p g s", s=1)[:, 0:G, :], in3)
        return v.tensor_tensor(
            bufA[:, 0:W].rearrange("p (g s) -> p g s", s=16), in3, inv_bc,
            ALU.mult)

    @k.c("scalar")
    def _(sc):
        return sc.activation(out=bufB[:, 0:W], in_=bufA[:, 0:W], func=AF.Abs)

    @k.c("vector")
    def _(v):
        a = bufB[:, 0:W]
        v.tensor_scalar(bufC[:, 0:W], a, FP4_MIDS[0], None, ALU.is_gt)
        for m in FP4_MIDS[1:4]:
            v.scalar_tensor_tensor(bufC[:, 0:W], a, m, bufC[:, 0:W],
                                   ALU.is_gt, ALU.add)
        v.tensor_scalar(bufA[:, 0:W], a, FP4_MIDS[6], 2.0, ALU.is_gt, ALU.mult)
        for m in (FP4_MIDS[4], FP4_MIDS[5]):
            v.scalar_tensor_tensor(bufA[:, 0:W], a, m, bufA[:, 0:W],
                                   ALU.is_gt, ALU.add)
        return v.scalar_tensor_tensor(bufA[:, 0:W], bufC[:, 0:W], 0.5,
                                      bufA[:, 0:W], ALU.mult, ALU.add)

    @k.c("scalar")
    def _(sc):
        return sc.activation(out=bufB[:, 0:W], in_=in_ap, func=AF.Sign)

    @k.c("vector")
    def _(v):
        b3 = bufB[:, 0:W].rearrange("p (g s) -> p g s", s=16)
        smax_bc = bcast_blocks(
            smax.rearrange("p (g s) -> p g s", s=1)[:, 0:G, :], b3)
        v.tensor_tensor(b3, b3, smax_bc, ALU.mult)
        return v.tensor_tensor(out_ap, bufA[:, 0:W], bufB[:, 0:W], ALU.mult)



FFN_I = 2816
NI = FFN_I // 128


def emit_ffn_fp4(k, cm, Wg, Wu, Wd, tag):
    wgq = k.dram(tag + "_wgq", [FFN_I, D], BF16)
    wuq = k.dram(tag + "_wuq", [FFN_I, D], BF16)
    wdq = k.dram(tag + "_wdq", [D, FFN_I], BF16)
    xnscr = k.dram(tag + "_xns", [D, TL], BF16)
    xqscr = k.dram(tag + "_xqs", [TL, D], BF16)
    hscr = k.dram(tag + "_hs", [FFN_I, TL], BF16)
    hqscr = k.dram(tag + "_hqs", [TL, FFN_I], BF16)

    wnat = k.sb(tag + "wnat", [128, FFN_I], BF16)
    qA = k.sb(tag + "qA", [128, FFN_I], BF16)
    qB = k.sb(tag + "qB", [128, FFN_I], BF16)
    qC = k.sb(tag + "qC", [128, FFN_I], BF16)
    qsr = k.sb(tag + "qsr", [128, FFN_I // 16], F32)
    qsm = k.sb(tag + "qsm", [128, FFN_I // 16], F32)
    xtm = k.sb(tag + "xtm", [128, D], BF16)
    xq_fm = k.sb(tag + "xqfm", [128, NC8 * 512], BF16)
    xn_fm = xq_fm
    htm = wnat
    wT = k.sb(tag + "wT", [128, 2 * NC8 * 256], BF16)
    sil = k.sb(tag + "sil", [128, 512], BF16)
    hfm = k.sb(tag + "hfm", [128, 512], BF16)
    hq_t = k.sb(tag + "hqt", [128, 512], BF16)
    wdTs = k.sb(tag + "wdTs", [128, 512], BF16)

    def quant_weight(Wsrc, Wdst, rows, cols):
        nsl_max = max(1, FFN_I // cols)
        r = 0
        while r < rows:
            nsl = min(nsl_max, (rows - r) // 128)
            W_ = cols * nsl

            @k.d("gpsimd")
            def _(s, r=r, nsl=nsl, W_=W_, Wsrc=Wsrc):
                src = Wsrc[r:r + 128 * nsl, :].rearrange("(a p) c -> p a c",
                                                         p=128)
                dst = wnat[:, 0:W_].rearrange("p (a c) -> p a c", a=nsl)
                return [s.dma_start(out=dst, in_=src)]

            emit_fp4_quant(k, wnat[:, 0:W_], qC[:, 0:W_], W_, qA, qB, qC,
                           qsr, qsm)

            @k.d("sync")
            def _(s, r=r, nsl=nsl, W_=W_, Wdst=Wdst):
                dst = Wdst[r:r + 128 * nsl, :].rearrange("(a p) c -> p a c",
                                                         p=128)
                src = qC[:, 0:W_].rearrange("p (a c) -> p a c", a=nsl)
                return [s.dma_start(out=dst, in_=src)]
            r += 128 * nsl

    quant_weight(Wg, wgq, FFN_I, D)
    quant_weight(Wu, wuq, FFN_I, D)
    quant_weight(Wd, wdq, D, FFN_I)

    for tt in range(TL // 512):
        cm.ln_stats_tile(tt * 512)
        cm.ln_bcast()
        cm.ln_apply(tt * 512, 512, xn_fm)

        @k.d("sync")
        def _(s, tt=tt):
            return [s.dma_start(
                out=xnscr[c * 128:(c + 1) * 128, tt * 512:(tt + 1) * 512],
                in_=xn_fm[:, c * 512:(c + 1) * 512]) for c in range(NC8)]

    for tt in range(TL // 128):
        @k.d("sync")
        def _(s, tt=tt):
            return [s.dma_start(out=xtm[:, :],
                                in_=xnscr[:, tt * 128:(tt + 1) * 128],
                                transpose=True)]
        emit_fp4_quant(k, xtm[:, :], qC[:, 0:D], D, qA, qB, qC, qsr, qsm)

        @k.d("sync")
        def _(s, tt=tt):
            return [s.dma_start(out=xqscr[tt * 128:(tt + 1) * 128, :],
                                in_=qC[:, 0:D])]

    n_groups = (NI + 1) // 2
    for tg in range(TL // 512):
        t0 = tg * 512

        @k.d("sync")
        def _(s, t0=t0):
            return [s.dma_start(out=xq_fm[:, c * 512:(c + 1) * 512],
                                in_=xqscr[t0:t0 + 512, c * 128:(c + 1) * 128],
                                transpose=True) for c in range(NC8)]
        for gI in range(n_groups):
            n0 = gI * 2
            gn = min(2, NI - n0)

            @k.d("sync")
            def _(s, n0=n0, gn=gn):
                ins = []
                for kc in range(NC8):
                    ins.append(s.dma_start(
                        out=wT[:, kc * 256: kc * 256 + gn * 128],
                        in_=wgq[n0 * 128:(n0 + gn) * 128,
                                kc * 128:(kc + 1) * 128], transpose=True))
                    ins.append(s.dma_start(
                        out=wT[:, NC8 * 256 + kc * 256: NC8 * 256 + kc * 256 + gn * 128],
                        in_=wuq[n0 * 128:(n0 + gn) * 128,
                                kc * 128:(kc + 1) * 128], transpose=True))
                return ins

            @k.c("tensor")
            def _(t, n0=n0, gn=gn):
                ins = None
                for j in range(gn):
                    for kc in range(NC8):
                        rhs = xq_fm[:, kc * 512:(kc + 1) * 512]
                        t.matmul(cm.pp[j][:, :],
                                 wT[:, kc * 256 + j * 128: kc * 256 + (j + 1) * 128],
                                 rhs, start=(kc == 0), stop=(kc == NC8 - 1))
                        ins = t.matmul(
                            cm.pp[3 + j][:, :],
                            wT[:, NC8 * 256 + kc * 256 + j * 128:
                               NC8 * 256 + kc * 256 + (j + 1) * 128],
                            rhs, start=(kc == 0), stop=(kc == NC8 - 1))
                return ins
            for j in range(gn):
                @k.c("scalar")
                def _(sc, j=j):
                    return sc.activation(out=sil[:, :], in_=cm.pp[j][:, :],
                                         func=AF.Silu)

                @k.c("vector")
                def _(v, j=j):
                    return v.tensor_tensor(hfm[:, :], sil[:, :],
                                           cm.pp[3 + j][:, :], ALU.mult)

                @k.d("sync")
                def _(s, n0=n0, j=j, t0=t0):
                    return [s.dma_start(
                        out=hscr[(n0 + j) * 128:(n0 + j + 1) * 128,
                                 t0: t0 + 512],
                        in_=hfm[:, :])]

    for tt in range(TL // 128):
        @k.d("sync")
        def _(s, tt=tt):
            return [s.dma_start(out=htm[:, :],
                                in_=hscr[:, tt * 128:(tt + 1) * 128],
                                transpose=True)]
        emit_fp4_quant(k, htm[:, :], qC[:, 0:FFN_I], FFN_I, qA, qB, qC,
                       qsr, qsm)

        @k.d("sync")
        def _(s, tt=tt):
            return [s.dma_start(out=hqscr[tt * 128:(tt + 1) * 128, :],
                                in_=qC[:, 0:FFN_I])]

    for tt in range(TL // 512):
        for dIg in range(2):
            for kc in range(NI):
                @k.d("sync")
                def _(s, tt=tt, kc=kc, dIg=dIg):
                    return [s.dma_start(out=hq_t[:, :],
                                        in_=hqscr[tt * 512:(tt + 1) * 512,
                                                  kc * 128:(kc + 1) * 128],
                                        transpose=True),
                            s.dma_start(out=wdTs[:, 0:512],
                                        in_=wdq[dIg * 512:(dIg + 1) * 512,
                                                kc * 128:(kc + 1) * 128],
                                        transpose=True)]

                @k.c("tensor")
                def _(t, kc=kc):
                    ins = None
                    for j in range(4):
                        ins = t.matmul(cm.pp[j][:, :],
                                       wdTs[:, j * 128:(j + 1) * 128],
                                       hq_t[:, :],
                                       start=(kc == 0), stop=(kc == NI - 1))
                    return ins

            @k.c("vector")
            def _(v, tt=tt, dIg=dIg):
                ins = None
                for j in range(4):
                    dI = dIg * 4 + j
                    sl = cm.strm[:, dI * TL + tt * 512: dI * TL + (tt + 1) * 512]
                    ins = v.tensor_tensor(sl, sl, cm.pp[j][:, :], ALU.add)
                return ins



def emit_l2_attn(k, cm, W, tag, stop_after="full"):
    kv_dram = k.dram(tag + "_kv", [64, 1024], F32)
    ks_dram = k.dram(tag + "_ks", [1, 1024], F32)
    kv_dram2 = k.dram(tag + "_kv2", [64, 1024], F32)
    ks_dram2 = k.dram(tag + "_ks2", [1, 1024], F32)

    y_dram = k.dram(tag + "_y", [D, 512], BF16)
    wA = k.sb(tag + "wA", [128, NC8 * 1024], BF16)
    wB = k.sb(tag + "wB", [128, NC8 * 1024], BF16)
    xn = k.sb(tag + "xn", [128, NC8 * 512], BF16)
    brow = k.sb(tag + "brow", [1, 1024], F32)
    bcols = k.sb(tag + "bcols", [128, 3 * NC8], F32)
    ksum = k.sb(tag + "ksum", [1, 1024], F32)
    kscol = k.sb(tag + "kscol", [128, NC8], F32)
    msb = k.sb(tag + "msb", [128, NC8 * 16], BF16)
    sel16 = k.sb(tag + "sel16", [16, NC8 * 128], BF16)
    m16 = k.sb(tag + "m16", [128, NC8 * 16], F32)
    fscb = k.sb(tag + "fscb", [128, 3 * 512], BF16)
    fsc = cm.sq[:, 0:1536]
    rbs = cm.sq[:, 1536:2048]
    k.push_scope()
    ktm = k.sb(tag + "ktm", [128, 1024], BF16)
    vtm = k.sb(tag + "vtm", [128, 1024], BF16)
    kvb = k.sb(tag + "kvb", [128, 2 * 1024], BF16)

    def fmap(pp_src, bias_bc_ap, out_ap):
        @k.c("vector")
        def _(v):
            v.tensor_copy(fscb[:, 0:512], pp_src)
            v.tensor_tensor(fscb[:, 0:512], fscb[:, 0:512], bias_bc_ap, ALU.add)
            return v.tensor_scalar_min(fscb[:, 512:1024], fscb[:, 0:512], 0.0)

        @k.c("scalar")
        def _(sc):
            return sc.activation(out=fscb[:, 1024:1536], in_=fscb[:, 512:1024],
                                 func=AF.Exp)

        @k.c("vector")
        def _(v):
            return v.scalar_tensor_tensor(out_ap, fscb[:, 0:512], 0.0,
                                          fscb[:, 1024:1536], ALU.max, ALU.add)

    def kv_off(h):
        return h * 64

    @k.d("gpsimd", wait=False)
    def _(g):
        return [g.dma_start(out=sel16[:, :], in_=W["selpair"]),
                g.dma_start(out=m16[:, :], in_=W["mask16"])]

    def brow_load(name):
        @k.d("sync")
        def _(s, name=name):
            return [s.dma_start(out=brow[0:1, :], in_=W[name][None, :])]

    brow_load("wq_b")
    cm.vec_to_cols(brow[0:1, :], bcols[:, 0:NC8], NC8, cm.pp[0])
    brow_load("wg_b")
    cm.vec_to_cols(brow[0:1, :], bcols[:, NC8:2 * NC8], NC8, cm.pp[0])
    brow_load("out_b")
    cm.vec_to_cols(brow[0:1, :], bcols[:, 2 * NC8:3 * NC8], NC8, cm.pp[0])
    brow_load("wk_b")
    cm.row_bcast(brow[0:1, :], kvb[:, 0:1024], 1024, cm.pp[1])
    brow_load("wv_b")
    cm.row_bcast(brow[0:1, :], kvb[:, 1024:2048], 1024, cm.pp[1])

    @k.d("gpsimd", wait=False)
    def _(g):
        ins = []
        for kc in range(NC8):
            ins.append(g.dma_start(out=wA[:, kc * 1024:(kc + 1) * 1024],
                                   in_=W["wk_w"][kc * 128:(kc + 1) * 128, :]))
            ins.append(g.dma_start(out=wB[:, kc * 1024:(kc + 1) * 1024],
                                   in_=W["wv_w"][kc * 128:(kc + 1) * 128, :]))
        return ins

    @k.c("vector", wait=False)
    def _(v):
        return v.memset(ksum[0:1, :], 0.0)

    if stop_after == "prep":
        return
    nts = TL // 128
    for tt in range(TL // 512):
        t0 = tt * 512
        cm.ln_stats_tile(t0)
        cm.ln_bcast()
        cm.ln_apply(t0, 512, xn)
        for ts in range(4):
            gi = tt * 4 + ts
            for jh in range(2):
                @k.c("tensor")
                def _(t, ts=ts, jh=jh):
                    ins = None
                    for kc in range(NC8):
                        sub = xn[:, kc * 512 + ts * 128: kc * 512 + (ts + 1) * 128]
                        t.matmul(cm.pp[0][:, :], sub,
                                 wA[:, kc * 1024 + jh * 512: kc * 1024 + (jh + 1) * 512],
                                 start=(kc == 0), stop=(kc == NC8 - 1))
                        ins = t.matmul(cm.pp[1][:, :], sub,
                                       wB[:, kc * 1024 + jh * 512: kc * 1024 + (jh + 1) * 512],
                                       start=(kc == 0), stop=(kc == NC8 - 1))
                    return ins
                fmap(cm.pp[0][:, :], kvb[:, jh * 512:(jh + 1) * 512],
                     ktm[:, jh * 512:(jh + 1) * 512])

                @k.c("vector")
                def _(v, jh=jh):
                    sl = vtm[:, jh * 512:(jh + 1) * 512]
                    v.tensor_copy(sl, cm.pp[1][:, :])
                    return v.tensor_tensor(
                        sl, sl, kvb[:, 1024 + jh * 512: 1024 + (jh + 1) * 512],
                        ALU.add)

            if stop_after == "p1nokv":
                continue

            @k.c("tensor")
            def _(t, gi=gi):
                ins = None
                for h in range(16):
                    pb = cm.pp[2] if h < 8 else cm.pp[3]
                    o = (h % 8) * 64
                    ins = t.matmul(pb[0:64, o:o + 64],
                                   ktm[:, h * 64:(h + 1) * 64],
                                   vtm[:, h * 64:(h + 1) * 64],
                                   start=(gi == 0 and h % 8 == 0),
                                   stop=(gi == nts - 1 and h % 8 == 7))
                return ins
            for jh in range(2):
                @k.c("tensor")
                def _(t, jh=jh):
                    return t.matmul(cm.pp[5][0:1, :], cm.ones_colb[:, 0:1],
                                    ktm[:, jh * 512:(jh + 1) * 512],
                                    start=True, stop=True)

                @k.c("vector")
                def _(v, jh=jh):
                    return v.tensor_tensor(ksum[0:1, jh * 512:(jh + 1) * 512],
                                           ksum[0:1, jh * 512:(jh + 1) * 512],
                                           cm.pp[5][0:1, :], ALU.add)

    k.pop_scope()
    woc = k.sb(tag + "woc", [128, 512], BF16)
    kvs = k.sb(tag + "kvs", [64, 1024], BF16)
    kv64 = k.sb(tag + "kv64", [128, 1024], BF16)
    qf = k.sb(tag + "qf", [128, 512], BF16)
    gat = k.sb(tag + "gat", [128, 512], F32)
    ydt = k.sb(tag + "ydt", [128, 512], BF16)
    rsb = k.sb(tag + "rsb", [16, 512], BF16)

    @k.c("vector")
    def _(v):
        v.tensor_copy(fsc[0:64, 0:512], cm.pp[2][0:64, :])
        return v.tensor_copy(fsc[0:64, 512:1024], cm.pp[3][0:64, :])

    @k.d("sync")
    def _(s):
        return [s.dma_start(out=kv_dram[:, :], in_=fsc[0:64, 0:1024]),
                s.dma_start(out=ks_dram[:, :], in_=ksum[0:1, :])]

    if stop_after == "pass1":
        return

    def coll(k2, blk, ph):
        total = ph.n

        def body(g):
            g.wait_ge(ph.h, total)
            g.collective_compute(
                "AllReduce", ALU.add, replica_groups=PAIRS,
                ins=[kv_dram.ap().opt()], outs=[kv_dram2.ap().opt()]).then_inc(ph.h, 1)
            g.collective_compute(
                "AllReduce", ALU.add, replica_groups=PAIRS,
                ins=[ks_dram.ap().opt()], outs=[ks_dram2.ap().opt()]).then_inc(ph.h, 1)
        blk.gpsimd(body)
        ph.n += 2
    k.manual(coll)

    @k.d("gpsimd")
    def _(g):
        return [g.dma_start(out=kvs[:, :], in_=kv_dram2[:, :]),
                g.dma_start(out=kv64[64:128, :], in_=kv_dram2[:, :]),
                g.dma_start(out=ksum[0:1, :], in_=ks_dram2[:, :])]

    cm.vec_to_cols(ksum[0:1, :], kscol[:, 0:NC8], NC8, cm.pp[0])

    @k.c("vector")
    def _(v):
        ks3 = kscol.rearrange("p (c o) -> p c o", o=1)
        m3 = m16.rearrange("p (c h) -> p c h", h=16)
        return v.tensor_tensor(msb.rearrange("p (c h) -> p c h", h=16),
                               m3, bcast_blocks(ks3, m3), ALU.mult)

    if stop_after == "coll":
        return

    @k.d("gpsimd")
    def _(g):
        ins = []
        for kc in range(NC8):
            ins.append(g.dma_start(out=wA[:, kc * 1024:(kc + 1) * 1024],
                                   in_=W["wq_w"][kc * 128:(kc + 1) * 128, :]))
            ins.append(g.dma_start(out=wB[:, kc * 1024:(kc + 1) * 1024],
                                   in_=W["wg_w"][kc * 128:(kc + 1) * 128, :]))
        return ins

    def q_chunk(c, dst):
        @k.c("tensor")
        def _(t, c=c):
            ins = None
            for kc in range(NC8):
                ins = t.matmul(cm.pp[0][:, :], wA[:, kc * 1024 + c * 128:
                                                  kc * 1024 + (c + 1) * 128],
                               xn[:, kc * 512:(kc + 1) * 512],
                               start=(kc == 0), stop=(kc == NC8 - 1))
            return ins

        @k.c("vector", )
        def _(v, c=c):
            v.tensor_scalar(fsc[:, 0:512], cm.pp[0][:, :], bcols[:, c:c + 1],
                            None, ALU.add)
            return v.tensor_scalar_min(fsc[:, 512:1024], fsc[:, 0:512], 0.0)

        @k.c("scalar")
        def _(sc):
            return sc.activation(out=fsc[:, 1024:1536], in_=fsc[:, 512:1024],
                                 func=AF.Exp)

        @k.c("vector")
        def _(v, dst=dst):
            return v.scalar_tensor_tensor(dst, fsc[:, 0:512], 0.0,
                                          fsc[:, 1024:1536], ALU.max, ALU.add)

    for tt in range(TL // 512):
        t0 = tt * 512
        cm.ln_stats_tile(t0)
        cm.ln_bcast()
        cm.ln_apply(t0, 512, xn)
        for c in range(NC8):
            q_chunk(c, qf[:, :])

            @k.c("tensor")
            def _(t, c=c):
                return t.matmul(cm.pp[2][0:16, :], msb[:, c * 16:(c + 1) * 16],
                                qf[:, :], start=(c == 0), stop=(c == NC8 - 1))

        @k.c("vector")
        def _(v):
            r32 = rbs[0:16, :]
            v.tensor_scalar(r32, cm.pp[2][0:16, :], 1e-6, None, ALU.add)
            v.reciprocal(r32, r32)
            return v.tensor_copy(rsb[:, :], r32)

        for c in range(NC8):
            q_chunk(c, qf[:, :])

            @k.c("tensor")
            def _(t, c=c):
                h0, h1 = 2 * c, 2 * c + 1
                t.matmul(cm.pp[3][0:64, :], kvs[0:64, kv_off(h0):kv_off(h0) + 64],
                         qf[0:64, :], start=True, stop=True)
                t.matmul(cm.pp[3][64:128, :],
                         kv64[64:128, kv_off(h1):kv_off(h1) + 64],
                         qf[64:128, :], start=True, stop=True)
                return t.matmul(cm.pp[4][:, :], sel16[:, c * 128:(c + 1) * 128],
                                rsb[:, :], start=True, stop=True)

            @k.c("tensor")
            def _(t, c=c):
                ins = None
                for kc in range(NC8):
                    ins = t.matmul(cm.pp[1][:, :], wB[:, kc * 1024 + c * 128:
                                                      kc * 1024 + (c + 1) * 128],
                                   xn[:, kc * 512:(kc + 1) * 512],
                                   start=(kc == 0), stop=(kc == NC8 - 1))
                return ins

            @k.c("scalar")
            def _(sc, c=c):
                return sc.activation(out=gat[:, :], in_=cm.pp[1][:, :],
                                     func=AF.Sigmoid,
                                     bias=bcols[:, NC8 + c:NC8 + c + 1])

            @k.c("vector")
            def _(v, c=c):
                v.tensor_copy(rbs[:, :], cm.pp[4][:, :])
                v.tensor_tensor(fsc[:, 0:512], cm.pp[3][:, :], rbs[:, :],
                                ALU.mult)
                return v.tensor_tensor(ydt[:, :], fsc[:, 0:512], gat[:, :],
                                       ALU.mult)

            @k.d("sync")
            def _(s, c=c):
                return [s.dma_start(out=y_dram[c * 128:(c + 1) * 128, :],
                                    in_=ydt[:, :])]

        for ocg in range(2):
            for kc in range(NC8):
                @k.d("gpsimd")
                def _(g, ocg=ocg, kc=kc):
                    return [g.dma_start(
                        out=woc[:, :],
                        in_=W["out_w"][kc * 128:(kc + 1) * 128,
                                       ocg * 512:(ocg + 1) * 512])]

                @k.d("sync")
                def _(s, kc=kc):
                    return [s.dma_start(out=ydt[:, :],
                                        in_=y_dram[kc * 128:(kc + 1) * 128, :])]

                @k.c("tensor")
                def _(t, kc=kc):
                    ins = None
                    for j in range(4):
                        ins = t.matmul(cm.pp[j][:, :],
                                       woc[:, j * 128:(j + 1) * 128],
                                       ydt[:, :],
                                       start=(kc == 0), stop=(kc == NC8 - 1))
                    return ins

            @k.c("vector")
            def _(v, ocg=ocg, t0=t0):
                ins = None
                for j in range(4):
                    oc = ocg * 4 + j
                    sl = cm.strm[:, oc * TL + t0: oc * TL + t0 + 512]
                    ins = v.scalar_tensor_tensor(
                        sl, cm.pp[j][:, :],
                        bcols[:, 2 * NC8 + oc:2 * NC8 + oc + 1],
                        sl, ALU.add, ALU.add)
                return ins



def emit_l1_ffn(k, cm, W, tag):
    NH = 4096
    sc_dram = k.dram(tag + "_sc", [1, TL], F32)
    sc_all = k.dram(tag + "_sca", [2, TL], F32)
    h_dram = k.dram(tag + "_h", [NH, 512], BF16)

    xn = k.sb(tag + "xn", [128, NC8 * 512], BF16)
    wr = k.sb(tag + "wr", [128, NC8], F32)
    sumw = k.sb(tag + "sumw", [1, NC8], F32)
    scores = k.sb(tag + "scores", [1, TL], F32)
    s128 = k.sb(tag + "s128", [128, 64], F32)
    gtb = k.sb(tag + "gtb", [128, 64], F32)
    cnt1 = k.sb(tag + "cnt1", [128, 1], F32)
    lo = k.sb(tag + "lo", [1, 8], F32)
    selrow = k.sb(tag + "selrow", [1, TL], F32)
    brow = k.sb(tag + "brow", [1, NH], F32)
    b1c = k.sb(tag + "b1c", [128, 32], F32)
    b2c = k.sb(tag + "b2c", [128, NC8], F32)
    br1 = k.sb(tag + "br1", [1, 1], F32)
    w1s = k.sb(tag + "w1s", [128, NC8 * 128], BF16)
    w2s = k.sb(tag + "w2s", [128, 512], BF16)
    hch = k.sb(tag + "hch", [128, 512], BF16)
    selbc = k.sb(tag + "selbc", [128, 512], F32)
    fsc = cm.sq[:, 0:1536]

    KTH = T // 2 - 1

    @k.d("sync", wait=False)
    def _(s):
        return [s.dma_start(out=brow[0:1, :], in_=W["b1"][None, :]),
                s.dma_start(out=br1[0:1, :], in_=W["b_r"][None, :])]

    cm.vec_to_cols(brow[0:1, :], b1c[:, 0:32], 32, cm.pp[0])

    @k.d("sync")
    def _(s):
        return [s.dma_start(out=brow[0:1, 0:D], in_=W["b2"][None, :])]

    cm.vec_to_cols(brow[0:1, 0:D], b2c[:, 0:NC8], NC8, cm.pp[0])

    @k.d("sync")
    def _(s):
        return [s.dma_start(out=brow[0:1, 0:D],
                            in_=W["w_r"].rearrange("c o -> o c"))]

    cm.vec_to_cols(brow[0:1, 0:D], wr[:, 0:NC8], NC8, cm.pp[0])

    @k.c("tensor")
    def _(t):
        return t.matmul(cm.psq[0:1, 0:NC8], cm.ones_col[:, 0:1], wr[:, :],
                        start=True, stop=True)

    @k.c("vector")
    def _(v):
        return v.reduce_sum(sumw[0:1, 0:1], cm.psq[0:1, 0:NC8], axis=AX.X)

    for tt in range(TL // 512):
        t0 = tt * 512
        cm.ln_stats_tile(t0)

        @k.c("tensor")
        def _(t, t0=t0):
            ins = None
            for kc in range(NC8):
                ins = t.matmul(cm.pmu[0:1, :], wr[:, kc:kc + 1],
                               cm.strm[:, kc * TL + t0: kc * TL + t0 + 512],
                               start=(kc == 0), stop=(kc == NC8 - 1))
            return ins

        @k.c("vector")
        def _(v, t0=t0):
            sl = scores[0:1, t0:t0 + 512]
            v.tensor_scalar(sl, cm.mu[:1, :], sumw[0:1, 0:1], None, ALU.mult)
            v.tensor_sub(sl, cm.pmu[0:1, :], sl)
            v.tensor_mul(sl, sl, cm.rs[:1, :])
            return v.tensor_scalar(sl, sl, br1[0:1, 0:1], None, ALU.add)

    @k.d("sync")
    def _(s):
        return [s.dma_start(out=sc_dram[:, :], in_=scores[0:1, :])]

    def coll(k2, blk, ph):
        total = ph.n

        def body(g):
            g.wait_ge(ph.h, total)
            g.collective_compute(
                "AllGather", ALU.bypass, replica_groups=PAIRS,
                ins=[sc_dram.ap().opt()], outs=[sc_all.ap().opt()]).then_inc(ph.h, 1)
        blk.gpsimd(body)
        ph.n += 1
    k.manual(coll)

    @k.d("sync")
    def _(s):
        return [s.dma_start(
            out=s128[:, :].rearrange("p (a c) -> p a c", a=2),
            in_=sc_all.ap().rearrange("a (p c) -> p a c", p=128))]

    @k.c("vector")
    def _(v):
        return v.reduce_max(cnt1[:, 0:1], s128[:, :], axis=AX.X,
                            apply_absolute_value=True)

    @k.c("tensor")
    def _(t):
        return t.matmul(cm.pmu[0:1, 0:1], cm.ones_col[:, 0:1], cnt1[:, 0:1],
                        start=True, stop=True)

    @k.c("vector")
    def _(v):
        v.tensor_copy(lo[0:1, 1:2], cm.pmu[0:1, 0:1])
        return v.tensor_scalar(lo[0:1, 0:1], cm.pmu[0:1, 0:1], -1.0, None,
                               ALU.mult)

    def vph(fn):
        @k.c("vector")
        def _(v, fn=fn):
            return fn(v)

    for it in range(40):
        vph(lambda v: v.tensor_tensor(lo[0:1, 2:3], lo[0:1, 0:1],
                                      lo[0:1, 1:2], ALU.add))
        vph(lambda v: v.tensor_scalar(lo[0:1, 2:3], lo[0:1, 2:3], 0.5, None,
                                      ALU.mult))

        @k.c("tensor")
        def _(t):
            return t.matmul(cm.pp[5][:, 0:1], cm.ones[0:1, :], lo[0:1, 2:3],
                            start=True, stop=True)

        vph(lambda v: v.tensor_copy(cnt1[:, 0:1], cm.pp[5][:, 0:1]))
        vph(lambda v: v.tensor_scalar(gtb[:, :], s128[:, :], cnt1[:, 0:1],
                                      None, ALU.is_gt))
        vph(lambda v: v.reduce_sum(cnt1[:, 0:1], gtb[:, :], axis=AX.X))

        @k.c("tensor")
        def _(t):
            return t.matmul(cm.pmu[0:1, 0:1], cm.ones_col[:, 0:1],
                            cnt1[:, 0:1], start=True, stop=True)

        vph(lambda v: v.tensor_scalar(lo[0:1, 5:6], cm.pmu[0:1, 0:1],
                                      float(KTH), None, ALU.is_gt))
        vph(lambda v: v.tensor_sub(lo[0:1, 4:5], lo[0:1, 2:3], lo[0:1, 0:1]))
        vph(lambda v: v.tensor_mul(lo[0:1, 4:5], lo[0:1, 4:5], lo[0:1, 5:6]))
        vph(lambda v: v.tensor_add(lo[0:1, 0:1], lo[0:1, 0:1], lo[0:1, 4:5]))
        vph(lambda v: v.tensor_scalar(lo[0:1, 5:6], lo[0:1, 5:6], -1.0, 1.0,
                                      ALU.mult, ALU.add))
        vph(lambda v: v.tensor_sub(lo[0:1, 4:5], lo[0:1, 2:3], lo[0:1, 1:2]))
        vph(lambda v: v.tensor_mul(lo[0:1, 4:5], lo[0:1, 4:5], lo[0:1, 5:6]))
        vph(lambda v: v.tensor_add(lo[0:1, 1:2], lo[0:1, 1:2], lo[0:1, 4:5]))

    @k.c("vector")
    def _(v):
        return v.tensor_scalar(selrow[0:1, :], scores[0:1, :], lo[0:1, 1:2],
                               None, ALU.is_gt)

    if "dbg_bis" in W:
        @k.d("sync")
        def _(s):
            return [s.dma_start(out=W["dbg_bis"], in_=lo[0:1, :]),
                    s.dma_start(out=W["dbg_scores"], in_=scores[0:1, :]),
                    s.dma_start(out=W["dbg_s128"], in_=s128[:, :])]

    for tt in range(TL // 512):
        t0 = tt * 512
        cm.ln_stats_tile(t0)
        cm.ln_bcast()
        cm.ln_apply(t0, 512, xn)
        for hI in range(32):
            @k.d("gpsimd")
            def _(g, hI=hI):
                return [g.dma_start(
                    out=w1s[:, :].rearrange("p (a c) -> p a c", a=NC8),
                    in_=W["W1"][:, hI * 128:(hI + 1) * 128]
                    .rearrange("(a p) c -> p a c", p=128))]

            @k.c("tensor")
            def _(t, hI=hI):
                ins = None
                for kc in range(NC8):
                    ins = t.matmul(cm.pp[0][:, :],
                                   w1s[:, kc * 128:(kc + 1) * 128],
                                   xn[:, kc * 512:(kc + 1) * 512],
                                   start=(kc == 0), stop=(kc == NC8 - 1))
                return ins

            @k.c("scalar")
            def _(sc, hI=hI):
                return sc.activation(out=hch[:, :], in_=cm.pp[0][:, :],
                                     func=AF.Gelu, bias=b1c[:, hI:hI + 1])

            @k.d("sync")
            def _(s, hI=hI):
                return [s.dma_start(out=h_dram[hI * 128:(hI + 1) * 128, :],
                                    in_=hch[:, :])]
        for ocg in range(2):
            for hI in range(32):
                @k.d("gpsimd")
                def _(g, ocg=ocg, hI=hI):
                    return [g.dma_start(out=w2s[:, :],
                                        in_=W["W2"][hI * 128:(hI + 1) * 128,
                                                    ocg * 512:(ocg + 1) * 512])]

                @k.d("sync")
                def _(s, hI=hI):
                    return [s.dma_start(out=hch[:, :],
                                        in_=h_dram[hI * 128:(hI + 1) * 128, :])]

                @k.c("tensor")
                def _(t, hI=hI):
                    ins = None
                    for j in range(4):
                        ins = t.matmul(cm.pp[j][:, :],
                                       w2s[:, j * 128:(j + 1) * 128],
                                       hch[:, :],
                                       start=(hI == 0), stop=(hI == 31))
                    return ins

            @k.c("tensor")
            def _(t, t0=t0):
                return t.matmul(cm.pp[4][:, :], cm.ones[0:1, :],
                                selrow[0:1, t0:t0 + 512], start=True, stop=True)

            @k.c("vector")
            def _(v, ocg=ocg, t0=t0):
                v.tensor_copy(selbc[:, :], cm.pp[4][:, :])
                ins = None
                for j in range(4):
                    oc = ocg * 4 + j
                    xnc = xn[:, oc * 512:(oc + 1) * 512]
                    sl = cm.strm[:, oc * TL + t0: oc * TL + t0 + 512]
                    v.tensor_scalar(fsc[:, 0:512], cm.pp[j][:, :],
                                    b2c[:, oc:oc + 1], None, ALU.add)
                    v.tensor_sub(fsc[:, 0:512], fsc[:, 0:512], xnc)
                    v.tensor_mul(fsc[:, 0:512], fsc[:, 0:512], selbc[:, :])
                    v.tensor_add(fsc[:, 0:512], fsc[:, 0:512], xnc)
                    ins = v.tensor_add(sl, sl, fsc[:, 0:512])
                return ins



def emit_l3_attn(k, cm, W, tag):
    qkvT_dram = k.dram(tag + "_qT", [192, D], BF16)
    pn_dram = k.dram(tag + "_pn", [D, 128], BF16)
    s_dram = k.dram(tag + "_s", [64, 64], F32)
    s_all = k.dram(tag + "_sa", [128, 64], F32)

    qkvT = k.sb(tag + "qkvT", [128, NC8 * 192], BF16)
    xn = k.sb(tag + "xn", [128, NC8 * 512], BF16)
    pT = k.sb(tag + "pT", [128, D], BF16)
    pnat = k.sb(tag + "pnat", [128, 1024], F32)
    pred = k.sb(tag + "pred", [128, 128], BF16)
    brow = k.sb(tag + "brow", [1, D], F32)
    bcols = k.sb(tag + "bcols", [128, 2], F32)
    obc = k.sb(tag + "obc", [128, NC8], F32)
    kvbc = k.sb(tag + "kvbc", [128, 2 * 64], F32)
    qf = k.sb(tag + "qf", [64, 128], BF16)
    kf = k.sb(tag + "kf", [64, 128], BF16)
    vt = k.sb(tag + "vt", [128, 64], BF16)
    kt = k.sb(tag + "kt", [128, 64], BF16)
    weif = k.sb(tag + "weif", [128, 128], F32)
    weib = k.sb(tag + "weib", [128, 128], BF16)
    ysb = k.sb(tag + "ysb", [64, 128], BF16)
    Ssb = k.sb(tag + "Ssb", [64, 64], F32)
    Sbf = k.sb(tag + "Sbf", [64, 64], BF16)
    Sin = k.sb(tag + "Sin", [64, 64], F32)

    @k.d("gpsimd", wait=False)
    def _(g):
        return [g.dma_start(out=qkvT_dram[:, :], in_=W["qkv_w"])]

    @k.d("sync")
    def _(s):
        return [s.dma_start(out=qkvT[:, kc * 192:(kc + 1) * 192],
                            in_=qkvT_dram[:, kc * 128:(kc + 1) * 128],
                            transpose=True) for kc in range(NC8)]

    for oc in range(NC8):
        @k.d("sync")
        def _(s, oc=oc):
            return [s.dma_start(out=pnat[:, :],
                                in_=W["proj_w"][oc * 128:(oc + 1) * 128, :])]

        @k.c("vector")
        def _(v):
            src = pnat.rearrange("p (r m) -> p m r", m=64)
            v.memset(pred[:, :], 0.0)
            v.reduce_sum(weif[:, 0:64].rearrange("p (m o) -> p m o", o=1),
                         src, axis=AX.X)
            return v.tensor_copy(pred[:, 0:64], weif[:, 0:64])

        @k.d("sync")
        def _(s, oc=oc):
            return [s.dma_start(out=pn_dram[oc * 128:(oc + 1) * 128, :],
                                in_=pred[:, :])]

    @k.d("sync")
    def _(s):
        return [s.dma_start(out=pT[:, :], in_=pn_dram[:, :], transpose=True)]

    @k.d("sync")
    def _(s):
        return [s.dma_start(out=brow[0:1, 0:192], in_=W["qkv_b"][None, :])]

    cm.vec_to_cols(brow[0:1, 0:128], bcols[:, 0:1], 1, cm.pp[0])

    @k.c("tensor")
    def _(t):
        return t.matmul(cm.pp[0][0:64, 0:1], brow[0:1, 64:128],
                        cm.one1[0:1, :], start=True, stop=True)

    @k.c("vector")
    def _(v):
        return v.tensor_copy(bcols[0:64, 1:2], cm.pp[0][0:64, 0:1])

    cm.row_bcast(brow[0:1, 64:128], kvbc[:, 0:64], 64, cm.pp[1])
    cm.row_bcast(brow[0:1, 128:192], kvbc[:, 64:128], 64, cm.pp[1])

    @k.d("sync")
    def _(s):
        return [s.dma_start(out=brow[0:1, 0:D], in_=W["proj_b"][None, :])]

    cm.vec_to_cols(brow[0:1, 0:D], obc[:, 0:NC8], NC8, cm.pp[0])

    @k.c("vector", wait=False)
    def _(v):
        return v.memset(Ssb[:, :], 0.0)

    for tt in range(TL // 512):
        t0 = tt * 512
        cm.ln_stats_tile(t0)
        cm.ln_bcast()
        cm.ln_apply(t0, 512, xn)
        for st4 in range(4):
            @k.c("tensor")
            def _(t, st4=st4):
                ins = None
                for kc in range(NC8):
                    sub = xn[:, kc * 512 + st4 * 128: kc * 512 + (st4 + 1) * 128]
                    t.matmul(cm.pp[0][:, 0:64], sub,
                             qkvT[:, kc * 192 + 64: kc * 192 + 128],
                             start=(kc == 0), stop=(kc == NC8 - 1))
                    ins = t.matmul(cm.pp[1][:, 0:64], sub,
                                   qkvT[:, kc * 192 + 128: kc * 192 + 192],
                                   start=(kc == 0), stop=(kc == NC8 - 1))
                return ins

            @k.c("vector")
            def _(v):
                v.tensor_tensor(kt[:, :], cm.pp[0][:, 0:64], kvbc[:, 0:64],
                                ALU.add)
                return v.tensor_tensor(vt[:, :], cm.pp[1][:, 0:64],
                                       kvbc[:, 64:128], ALU.add)

            @k.c("tensor")
            def _(t):
                return t.matmul(cm.pp[2][0:64, 0:64], kt[:, :], vt[:, :],
                                start=True, stop=True)

            @k.c("vector")
            def _(v):
                return v.tensor_tensor(Ssb[:, :], Ssb[:, :],
                                       cm.pp[2][0:64, 0:64], ALU.add)

    @k.d("sync")
    def _(s):
        return [s.dma_start(out=s_dram[:, :], in_=Ssb[:, :])]

    def coll(k2, blk, ph):
        total = ph.n

        def body(g):
            g.wait_ge(ph.h, total)
            g.collective_compute(
                "AllGather", ALU.bypass, replica_groups=PAIRS,
                ins=[s_dram.ap().opt()], outs=[s_all.ap().opt()]).then_inc(ph.h, 1)
        blk.gpsimd(body)
        ph.n += 1
    k.manual(coll)

    @k.d("sync")
    def _(s):
        return [s.dma_start(out=Sin[:, :], in_=s_all[0:64, :])]

    @k.c("vector")
    def _(v):
        v.tensor_scalar(Sin[:, :], Sin[:, :], cm.flagcol[0:64, 0:1], None,
                        ALU.mult)
        return v.tensor_copy(Ssb[:, :], Sin[:, :])

    @k.c("vector")
    def _(v):
        return v.tensor_copy(Sbf[:, :], Ssb[:, :])

    for tt in range(TL // 512):
        t0 = tt * 512
        cm.ln_stats_tile(t0)
        cm.ln_bcast()
        cm.ln_apply(t0, 512, xn)
        for st4 in range(4):
            @k.c("tensor")
            def _(t, st4=st4):
                ins = None
                for kc in range(NC8):
                    sub = xn[:, kc * 512 + st4 * 128: kc * 512 + (st4 + 1) * 128]
                    t.matmul(cm.pp[0][0:64, 0:128],
                             qkvT[:, kc * 192: kc * 192 + 64],
                             sub, start=(kc == 0), stop=(kc == NC8 - 1))
                    t.matmul(cm.pp[1][0:64, 0:128],
                             qkvT[:, kc * 192 + 64: kc * 192 + 128],
                             sub, start=(kc == 0), stop=(kc == NC8 - 1))
                    t.matmul(cm.pp[2][:, 0:64], sub,
                             qkvT[:, kc * 192 + 64: kc * 192 + 128],
                             start=(kc == 0), stop=(kc == NC8 - 1))
                    ins = t.matmul(cm.pp[3][:, 0:64], sub,
                                   qkvT[:, kc * 192 + 128: kc * 192 + 192],
                                   start=(kc == 0), stop=(kc == NC8 - 1))
                return ins

            @k.c("vector")
            def _(v):
                v.tensor_scalar(qf[:, :], cm.pp[0][0:64, 0:128],
                                bcols[0:64, 0:1], None, ALU.add)
                v.tensor_scalar(kf[:, :], cm.pp[1][0:64, 0:128],
                                bcols[0:64, 1:2], None, ALU.add)
                v.tensor_tensor(kt[:, :], cm.pp[2][:, 0:64], kvbc[:, 0:64],
                                ALU.add)
                return v.tensor_tensor(vt[:, :], cm.pp[3][:, 0:64],
                                       kvbc[:, 64:128], ALU.add)

            @k.c("tensor")
            def _(t):
                return t.matmul(cm.pp[4][:, 0:128], kf[:, :], qf[:, :],
                                start=True, stop=True)

            @k.c("vector")
            def _(v):
                return v.tensor_copy(weif[:, :], cm.pp[4][:, 0:128])

            @k.c("gpsimd")
            def _(g):
                return g.affine_select(weib[:, :], weif[:, :], [[1, 128]],
                                       ALU.is_ge, 0.0, base=0,
                                       channel_multiplier=-1)

            @k.c("tensor")
            def _(t):
                t.matmul(cm.pp[5][0:64, 0:128], vt[:, :], weib[:, :],
                         start=True, stop=False)
                return t.matmul(cm.pp[5][0:64, 0:128], Sbf[:, :], qf[:, :],
                                start=False, stop=True)

            @k.c("vector")
            def _(v):
                return v.tensor_copy(ysb[:, :], cm.pp[5][0:64, 0:128])

            @k.c("tensor")
            def _(t):
                return t.matmul(cm.pp[4][0:64, 0:64], kt[:, :], vt[:, :],
                                start=True, stop=True)

            @k.c("vector")
            def _(v):
                v.tensor_tensor(Ssb[:, :], Ssb[:, :], cm.pp[4][0:64, 0:64],
                                ALU.add)
                return v.tensor_copy(Sbf[:, :], Ssb[:, :])

            for og in range(4):
                @k.c("tensor")
                def _(t, og=og):
                    t.matmul(cm.pp[2][:, 0:128],
                             pT[0:64, (2 * og) * 128:(2 * og + 1) * 128],
                             ysb[:, :], start=True, stop=True)
                    return t.matmul(cm.pp[3][:, 0:128],
                                    pT[0:64, (2 * og + 1) * 128:(2 * og + 2) * 128],
                                    ysb[:, :], start=True, stop=True)

                @k.c("vector")
                def _(v, og=og, tt=tt, st4=st4):
                    t0s = tt * 512 + st4 * 128
                    ins = None
                    for j in range(2):
                        oc = 2 * og + j
                        sl = cm.strm[:, oc * TL + t0s: oc * TL + t0s + 128]
                        ins = v.scalar_tensor_tensor(
                            sl, cm.pp[2 + j][:, 0:128], obc[:, oc:oc + 1],
                            sl, ALU.add, ALU.add)
                    return ins



def emit_l2_ffn(k, cm, W, tag):
    NH2 = 4096
    xn_dram = k.dram(tag + "_xn", [D, TL], BF16)
    m_dram = k.dram(tag + "_m", [NH2, TL], BF16)
    w1b = k.dram(tag + "_w1b", [8, D, 2 * NH2], BF16)
    w2b = k.dram(tag + "_w2b", [8, NH2, D], BF16)

    mfm = k.sb(tag + "mfm", [8, TL], BF16)
    inde = k.sb(tag + "inde", [8, 8 * 128], BF16)
    ident = k.sb(tag + "ident", [128, 128], F32)
    mbc = k.sb(tag + "mbc", [128, 512], BF16)
    fsc = cm.sq[:, 0:512]
    k.push_scope()
    xnt = k.sb(tag + "xnt", [128, NC8 * 512], BF16)
    rw = k.sb(tag + "rw", [128, NC8 * 8], F32)
    rbb = k.sb(tag + "rbb", [128, 8], F32)
    wsb = k.sb(tag + "wsb", [128, 8], F32)
    brow = k.sb(tag + "brow", [1, 1024], F32)
    lgt = k.sb(tag + "lgt", [128, 8], F32)
    sm1 = k.sb(tag + "sm1", [128, 8], F32)
    sme = k.sb(tag + "sme", [128, 8], F32)
    smz = k.sb(tag + "smz", [128, 8], F32)
    rmx = k.sb(tag + "rmx", [128, 1], F32)
    esm = k.sb(tag + "esm", [128, 1], F32)
    mtk = k.sb(tag + "mtk", [128, 8], F32)
    zer8 = k.sb(tag + "zer8", [128, 8], F32)

    @k.c("gpsimd")
    def _(g):
        g.affine_select(ident[:, :], cm.ones[:, :], [[1, 128]], ALU.is_equal,
                        0.0, base=0, channel_multiplier=-1)
        ins = None
        for e in range(8):
            ins = g.affine_select(inde[0:8, e * 128:(e + 1) * 128],
                                  cm.ones[0:8, :], [[0, 128]], ALU.is_equal,
                                  0.0, base=-e, channel_multiplier=1)
        return ins

    @k.c("vector", wait=False)
    def _(v):
        return v.memset(zer8[:, :], 0.0)

    @k.d("sync", wait=False)
    def _(s):
        return [s.dma_start(out=rw[:, :].rearrange("p (c e) -> p c e", c=NC8),
                            in_=W["router_w"].rearrange("(c p) e -> p c e", p=128)),
                s.dma_start(out=brow[0:1, 0:8], in_=W["router_b"][None, :])]

    @k.c("tensor")
    def _(t):
        t.matmul(cm.pp[0][:, 0:8], cm.ones[0:1, :], brow[0:1, 0:8],
                 start=True, stop=True)
        return t.matmul(cm.pp[1][0:1, 0:8 * NC8], cm.ones_col[:, 0:1],
                        rw[:, :], start=True, stop=True)

    @k.c("vector")
    def _(v):
        v.tensor_copy(rbb[:, :], cm.pp[0][:, 0:8])
        s3 = cm.pp[1][0:1, 0:8 * NC8].rearrange("p (c e) -> p e c", c=NC8)
        return v.reduce_sum(brow[0:1, 8:16].rearrange("p (e o) -> p e o", o=1),
                            s3, axis=AX.X)

    @k.c("tensor")
    def _(t):
        return t.matmul(cm.pp[2][:, 0:8], cm.ones[0:1, :], brow[0:1, 8:16],
                        start=True, stop=True)

    @k.c("vector")
    def _(v):
        return v.tensor_copy(wsb[:, :], cm.pp[2][:, 0:8])

    @k.d("gpsimd", wait=False)
    def _(g):
        ins = []
        for e in range(8):
            ins.append(g.dma_start(out=w1b[e, :, :], in_=W["w1"][e, :, :]))
            ins.append(g.dma_start(out=w2b[e, :, :], in_=W["w2"][e, :, :]))
        return ins

    for tt in range(TL // 512):
        t0 = tt * 512
        cm.ln_stats_tile(t0)
        cm.ln_bcast()
        cm.ln_apply(t0, 512, xnt)

        @k.d("sync")
        def _(s, t0=t0):
            return [s.dma_start(out=xn_dram[c * 128:(c + 1) * 128, t0:t0 + 512],
                                in_=xnt[:, c * 512:(c + 1) * 512])
                    for c in range(NC8)]
        for st4 in range(4):
            ts0 = t0 + st4 * 128

            @k.c("tensor")
            def _(t, t0=t0, st4=st4):
                for kc in range(NC8):
                    t.matmul(cm.pp[0][:, 0:8],
                             cm.strm[:, kc * TL + t0 + st4 * 128:
                                     kc * TL + t0 + (st4 + 1) * 128],
                             rw[:, kc * 8:(kc + 1) * 8],
                             start=(kc == 0), stop=(kc == NC8 - 1))
                t.matmul(cm.pp[1][:, 0:1], cm.mu[0:1, st4 * 128:(st4 + 1) * 128],
                         cm.one1[0:1, :], start=True, stop=True)
                return t.matmul(cm.pp[1][:, 1:2],
                                cm.rs[0:1, st4 * 128:(st4 + 1) * 128],
                                cm.one1[0:1, :], start=True, stop=True)

            def vph(fn):
                @k.c("vector")
                def _(v, fn=fn):
                    return fn(v)
            vph(lambda v: v.tensor_scalar(lgt[:, :], wsb[:, :],
                                          cm.pp[1][:, 0:1], None, ALU.mult))
            vph(lambda v: v.tensor_tensor(lgt[:, :], cm.pp[0][:, 0:8],
                                          lgt[:, :], ALU.subtract))
            vph(lambda v: v.tensor_scalar(lgt[:, :], lgt[:, :],
                                          cm.pp[1][:, 1:2], None, ALU.mult))
            vph(lambda v: v.tensor_tensor(lgt[:, :], lgt[:, :], rbb[:, :],
                                          ALU.add))
            vph(lambda v: v.reduce_max(rmx[:, 0:1], lgt[:, :], axis=AX.X,
                                       negate=True))

            @k.c("scalar")
            def _(sc):
                return sc.activation(out=sme[:, :], in_=lgt[:, :], func=AF.Exp,
                                     bias=rmx[:, 0:1], accum_out=esm[:, 0:1])

            vph(lambda v: v.reciprocal(esm[:, 0:1], esm[:, 0:1]))
            vph(lambda v: v.tensor_scalar(sm1[:, :], lgt[:, :], rmx[:, 0:1],
                                          None, ALU.add))
            vph(lambda v: v.tensor_scalar(sm1[:, :], sm1[:, :], 0.0, None,
                                          ALU.is_ge))
            vph(lambda v: v.tensor_tensor_scan(smz[:, :], sm1[:, :],
                                               zer8[:, :], 0.0, ALU.add,
                                               ALU.add))
            vph(lambda v: v.tensor_scalar(smz[:, :], smz[:, :], 1.0, None,
                                          ALU.is_equal))
            vph(lambda v: v.tensor_tensor(sm1[:, :], sm1[:, :], smz[:, :],
                                          ALU.mult))
            vph(lambda v: v.tensor_scalar(mtk[:, :], sm1[:, :], esm[:, 0:1],
                                          None, ALU.mult))

            @k.c("tensor")
            def _(t):
                return t.transpose(cm.pp[3][0:8, 0:128], mtk[:, :], ident[:, :])

            @k.c("vector")
            def _(v, ts0=ts0):
                return v.tensor_copy(mfm[0:8, ts0:ts0 + 128],
                                     cm.pp[3][0:8, 0:128])

    k.pop_scope()
    for e in range(8):
      for th in range(2):
        t0h = th * 2048
        k.push_scope()
        xnr = k.sb(f"{tag}xnr{e}_{th}", [128, NC8 * 2048], BF16)
        w1s = k.sb(f"{tag}w1s{e}_{th}", [128, 2 * NC8 * 128], BF16)
        sil = k.sb(f"{tag}sil{e}_{th}", [128, 512], BF16)
        mst = k.sb(f"{tag}mst{e}_{th}", [128, NC8 * 512], BF16)

        @k.d("sync")
        def _(s, t0h=t0h):
            return [s.dma_start(
                out=xnr[:, c * 2048:(c + 1) * 2048],
                in_=xn_dram[c * 128:(c + 1) * 128, t0h:t0h + 2048])
                for c in range(NC8)]

        for nc2 in range(32):
            @k.d("sync")
            def _(s, e=e, nc2=nc2):
                return [s.dma_start(
                    out=w1s[:, 0:NC8 * 128].rearrange("p (a c) -> p a c", a=NC8),
                    in_=w1b[e, :, nc2 * 128:(nc2 + 1) * 128]
                    .rearrange("(a p) c -> p a c", p=128)),
                    s.dma_start(
                    out=w1s[:, NC8 * 128:].rearrange("p (a c) -> p a c", a=NC8),
                    in_=w1b[e, :, NH2 + nc2 * 128: NH2 + (nc2 + 1) * 128]
                    .rearrange("(a p) c -> p a c", p=128))]
            for tt in range(4):
                @k.c("tensor")
                def _(t, tt=tt):
                    ins = None
                    for kc in range(NC8):
                        sub = xnr[:, kc * 2048 + tt * 512: kc * 2048 + (tt + 1) * 512]
                        t.matmul(cm.pp[0][:, :],
                                 w1s[:, kc * 128:(kc + 1) * 128], sub,
                                 start=(kc == 0), stop=(kc == NC8 - 1))
                        ins = t.matmul(cm.pp[1][:, :],
                                       w1s[:, NC8 * 128 + kc * 128:
                                           NC8 * 128 + (kc + 1) * 128], sub,
                                       start=(kc == 0), stop=(kc == NC8 - 1))
                    return ins

                @k.c("scalar")
                def _(sc):
                    return sc.activation(out=sil[:, :], in_=cm.pp[0][:, :],
                                         func=AF.Silu)

                @k.c("vector")
                def _(v, tt=tt):
                    return v.tensor_tensor(mst[:, tt * 512:(tt + 1) * 512],
                                           sil[:, :], cm.pp[1][:, :], ALU.mult)

            @k.d("sync")
            def _(s, nc2=nc2, t0h=t0h):
                return [s.dma_start(
                    out=m_dram[nc2 * 128:(nc2 + 1) * 128, t0h:t0h + 2048],
                    in_=mst[:, 0:2048])]
        k.pop_scope()
      if True:

        k.push_scope()
        w2h = k.sb(f"{tag}w2h{e}", [128, 32 * 512], BF16)
        mty = k.sb(f"{tag}mty{e}", [128, 16 * 512], BF16)
        for ocg in range(2):
            @k.d("sync")
            def _(s, e=e, ocg=ocg):
                return [s.dma_start(
                    out=w2h[:, :].rearrange("p (a c) -> p a c", a=32),
                    in_=w2b[e, :, ocg * 512:(ocg + 1) * 512]
                    .rearrange("(a p) c -> p a c", p=128))]
            for tt in range(TL // 512):
                for mg in range(2):
                    @k.d("sync")
                    def _(s, tt=tt, mg=mg):
                        return [s.dma_start(
                            out=mty[:, :].rearrange("p (a c) -> p a c", a=16),
                            in_=m_dram[mg * 2048:(mg + 1) * 2048,
                                       tt * 512:(tt + 1) * 512]
                            .rearrange("(a p) c -> p a c", p=128))]

                    @k.c("tensor")
                    def _(t, mg=mg):
                        ins = None
                        for hI in range(16):
                            for j in range(4):
                                ins = t.matmul(
                                    cm.pp[j][:, :],
                                    w2h[:, (mg * 16 + hI) * 512 + j * 128:
                                        (mg * 16 + hI) * 512 + (j + 1) * 128],
                                    mty[:, hI * 512:(hI + 1) * 512],
                                    start=(mg == 0 and hI == 0),
                                    stop=(mg == 1 and hI == 15))
                        return ins

                @k.c("tensor")
                def _(t, e=e, tt=tt):
                    return t.matmul(cm.pp[4][:, :],
                                    inde[0:8, e * 128:(e + 1) * 128],
                                    mfm[0:8, tt * 512:(tt + 1) * 512],
                                    start=True, stop=True)

                @k.c("vector")
                def _(v, ocg=ocg, tt=tt):
                    v.tensor_copy(mbc[:, :], cm.pp[4][:, :])
                    ins = None
                    for j in range(4):
                        oc = ocg * 4 + j
                        sl = cm.strm[:, oc * TL + tt * 512:
                                     oc * TL + (tt + 1) * 512]
                        v.tensor_tensor(fsc[:, 0:512], cm.pp[j][:, :],
                                        mbc[:, :], ALU.mult)
                        ins = v.tensor_add(sl, sl, fsc[:, 0:512])
                    return ins
        k.pop_scope()



def emit_l1_attn(k, cm, W, tag):
    NDI = 2048
    NDC = 16
    wxb = k.dram(tag + "_wxb", [NDI, D], BF16)
    wzb = k.dram(tag + "_wzb", [NDI, D], BF16)
    wxpb = k.dram(tag + "_wxpb", [96, NDI], BF16)
    wdtb = k.dram(tag + "_wdtb", [NDI, 128], BF16)
    woutb = k.dram(tag + "_wob", [D, NDI], BF16)
    xb_dram = k.dram(tag + "_xb", [NDI, TL], BF16)
    xbs_dram = k.dram(tag + "_xbs", [NDI, TL], BF16)
    z_dram = k.dram(tag + "_z", [NDI, TL], BF16)
    dt_dram = k.dram(tag + "_dt", [NDI, TL], BF16)
    y_dram = k.dram(tag + "_y", [NDI, TL], BF16)
    xn_dram = k.dram(tag + "_xn", [D, TL], BF16)
    tail_d = k.dram(tag + "_tl", [NDI, 4], F32)
    tail_a = k.dram(tag + "_tla", [2 * NDI, 4], F32)
    hend_d = k.dram(tag + "_he", [NDI, 16], F32)
    hend_a = k.dram(tag + "_hea", [2 * NDI, 16], F32)

    nwc = k.sb(tag + "nwc", [128, NC8], F32)
    bxc = k.sb(tag + "bxc", [128, NDC], F32)
    bzc = k.sb(tag + "bzc", [128, NDC], F32)
    bdtc = k.sb(tag + "bdtc", [128, NDC], F32)
    cbc = k.sb(tag + "cbc", [128, NDC], F32)
    dcl = k.sb(tag + "dcl", [128, NDC], F32)
    boc = k.sb(tag + "boc", [128, NC8], F32)
    acl = k.sb(tag + "acl", [128, NDC * 16], F32)
    wcv = k.sb(tag + "wcv", [128, NDC * 4], F32)
    brow = k.sb(tag + "brow", [1, 1024], F32)
    selB = k.sb(tag + "selB", [16, 16 * 128], BF16)
    Bf = k.sb(tag + "Bf", [16, TL], BF16)
    Cf = k.sb(tag + "Cf", [16, TL], BF16)
    hin = k.sb(tag + "hin", [128, NDC * 16], F32)
    zb16 = k.sb(tag + "zb16", [128, 1024], BF16)
    fsc = cm.sq[:, 0:512]

    @k.d("gpsimd", wait=False)
    def _(g):
        return [g.dma_start(out=wxb[:, :], in_=W["Wx"]),
                g.dma_start(out=wzb[:, :], in_=W["Wz"]),
                g.dma_start(out=wxpb[:, :], in_=W["Wxp"]),
                g.dma_start(out=wdtb[:, 0:64], in_=W["Wdt"]),
                g.dma_start(out=woutb[:, :], in_=W["Wout"])]

    def row_to_cols(name_ap, dst, n):
        done = 0
        while done < n:
            g = min(8, n - done)

            @k.d("sync")
            def _(s, done=done, g=g):
                return [s.dma_start(out=brow[0:1, 0:g * 128],
                                    in_=name_ap[None, done * 128:
                                                (done + g) * 128])]
            cm.vec_to_cols(brow[0:1, 0:g * 128], dst[:, done:done + g], g,
                           cm.pp[0])
            done += g

    row_to_cols(W["norm_w"], nwc[:, 0:NC8], NC8)
    row_to_cols(W["bx"], bxc[:, :], NDC)
    row_to_cols(W["bz"], bzc[:, :], NDC)
    row_to_cols(W["bdt"], bdtc[:, :], NDC)
    row_to_cols(W["conv_b"], cbc[:, :], NDC)
    row_to_cols(W["D"], dcl[:, :], NDC)
    row_to_cols(W["bout"], boc[:, 0:NC8], NC8)

    @k.d("sync")
    def _(s):
        return [s.dma_start(
            out=acl[:, :].rearrange("p (c n) -> p c n", c=NDC),
            in_=W["A_log"].rearrange("(c p) n -> p c n", p=128)),
            s.dma_start(
            out=wcv[:, :].rearrange("p (c j) -> p c j", c=NDC),
            in_=W["conv_w"].rearrange("(c p) o j -> p c (o j)", p=128)
            .rearrange("p c j -> p c j"))]

    @k.c("vector", wait=False)
    def _(v):
        return v.memset(zb16[:, :], 0.0)

    @k.d("sync")
    def _(s):
        return [s.dma_start(
            out=wdtb[:, 64:128].rearrange("(a p) c -> p a c", p=128),
            in_=zb16[:, :].rearrange("p (a c) -> p a c", c=64))]

    @k.c("scalar")
    def _(sc):
        return sc.activation(out=acl[:, :], in_=acl[:, :], func=AF.Exp)

    @k.c("vector")
    def _(v):
        return v.tensor_scalar(acl[:, :], acl[:, :], -1.0, None, ALU.mult)

    @k.c("gpsimd")
    def _(g):
        ins = None
        for n in range(16):
            ins = g.affine_select(selB[0:16, n * 128:(n + 1) * 128],
                                  cm.ones[0:16, :], [[0, 128]], ALU.is_equal,
                                  0.0, base=-n, channel_multiplier=1)
        return ins

    k.push_scope()
    xr = k.sb(tag + "xr", [128, NC8 * 512], BF16)
    wT = k.sb(tag + "wT", [128, NC8 * 1024], BF16)
    stg = k.sb(tag + "stg", [128, 8 * 512], BF16)

    def ln_xr(t0):
        cm.ln_stats_tile(t0, rms_fuse=True)
        cm.ln_bcast()

        @k.c("vector")
        def _(v, t0=t0):
            ins = None
            for i in range(NC8):
                sl_in = cm.strm[:, i * TL + t0: i * TL + t0 + 512]
                tmp = cm.sq[:, (i % 4) * 512:(i % 4) * 512 + 512]
                v.tensor_sub(tmp, sl_in, cm.pp[4][:, 0:512])
                v.tensor_mul(tmp, tmp, cm.pp[5][:, 0:512])
                ins = v.tensor_scalar(xr[:, i * 512:(i + 1) * 512], tmp,
                                      nwc[:, i:i + 1], None, ALU.mult)
            return ins

    for wsel, (wsrc, bcol, dst_dram) in enumerate(
            [(wxb, bxc, xb_dram), (wzb, bzc, z_dram)]):
      for ih in range(2):
        i0 = ih * 8

        @k.d("sync")
        def _(s, wsrc=wsrc, i0=i0):
            return [s.dma_start(out=wT[:, kc * 1024:(kc + 1) * 1024],
                                in_=wsrc[i0 * 128:(i0 + 8) * 128,
                                         kc * 128:(kc + 1) * 128],
                                transpose=True) for kc in range(NC8)]
        for tt in range(TL // 512):
            t0 = tt * 512
            ln_xr(t0)
            if wsel == 0 and ih == 0:
                @k.d("sync")
                def _(s, t0=t0):
                    return [s.dma_start(
                        out=xn_dram[c * 128:(c + 1) * 128, t0:t0 + 512],
                        in_=xr[:, c * 512:(c + 1) * 512]) for c in range(NC8)]
            for ii in range(8):
                @k.c("tensor")
                def _(t, ii=ii):
                    ins = None
                    for kc in range(NC8):
                        ins = t.matmul(cm.pp[ii % 2][:, :],
                                       wT[:, kc * 1024 + ii * 128:
                                          kc * 1024 + (ii + 1) * 128],
                                       xr[:, kc * 512:(kc + 1) * 512],
                                       start=(kc == 0), stop=(kc == NC8 - 1))
                    return ins

                @k.c("vector")
                def _(v, ii=ii, bcol=bcol, i0=i0):
                    return v.tensor_scalar(stg[:, ii * 512:(ii + 1) * 512],
                                           cm.pp[ii % 2][:, :],
                                           bcol[:, i0 + ii:i0 + ii + 1], None,
                                           ALU.add)

            @k.d("sync")
            def _(s, t0=t0, dst_dram=dst_dram, i0=i0):
                return [s.dma_start(
                    out=dst_dram[i0 * 128:(i0 + 8) * 128, t0:t0 + 512]
                    .rearrange("(a p) c -> p a c", p=128),
                    in_=stg[:, :].rearrange("p (a c) -> p a c", a=8))]
    k.pop_scope()


    @k.d("gpsimd")
    def _(g):
        return [g.dma_start(out=tail_d[:, 0:4],
                            in_=xb_dram[:, TL - 4:TL])]

    def coll1(k2, blk, ph):
        total = ph.n

        def body(g):
            g.wait_ge(ph.h, total)
            g.collective_compute(
                "AllGather", ALU.bypass, replica_groups=PAIRS,
                ins=[tail_d.ap().opt()], outs=[tail_a.ap().opt()]).then_inc(ph.h, 1)
        blk.gpsimd(body)
        ph.n += 1
    k.manual(coll1)

    k.push_scope()
    xbc = k.sb(tag + "xbc", [128, TL], BF16)
    halo = k.sb(tag + "halo", [128, 4], F32)
    cacc = k.sb(tag + "cacc", [128, TL], F32)

    for dc in range(NDC):
        @k.d("sync")
        def _(s, dc=dc):
            return [s.dma_start(out=xbc[:, :],
                                in_=xb_dram[dc * 128:(dc + 1) * 128, :]),
                    s.dma_start(out=halo[:, 0:4],
                                in_=tail_a[dc * 128:(dc + 1) * 128, :])]

        @k.c("vector")
        def _(v, dc=dc):
            v.tensor_scalar(halo[:, :], halo[:, :], cm.flagcol[:, 0:1], None,
                            ALU.mult)
            v.tensor_scalar(cacc[:, :], xbc[:, :], wcv[:, dc * 4 + 3:dc * 4 + 4],
                            None, ALU.mult)
            for j in range(3):
                sh = 3 - j
                v.scalar_tensor_tensor(cacc[:, sh:TL], xbc[:, 0:TL - sh],
                                       wcv[:, dc * 4 + j:dc * 4 + j + 1],
                                       cacc[:, sh:TL], ALU.mult, ALU.add)
            return v.tensor_scalar(cacc[:, :], cacc[:, :], cbc[:, dc:dc + 1],
                                   None, ALU.add)

        for p in range(3):
            for j in range(3 - p):
                @k.c("vector")
                def _(v, dc=dc, p=p, j=j):
                    hcol = halo[:, 1 + p + j: 2 + p + j]
                    return v.scalar_tensor_tensor(
                        cacc[:, p:p + 1], hcol,
                        wcv[:, dc * 4 + j:dc * 4 + j + 1],
                        cacc[:, p:p + 1], ALU.mult, ALU.add)

        @k.c("scalar")
        def _(sc):
            return sc.activation(out=xbc[:, :], in_=cacc[:, :], func=AF.Silu)

        @k.d("sync")
        def _(s, dc=dc):
            return [s.dma_start(out=xbs_dram[dc * 128:(dc + 1) * 128, :],
                                in_=xbc[:, :])]
    k.pop_scope()

    k.push_scope()
    dtl = k.sb(tag + "dtl", [64, TL], BF16)
    xdbi = k.sb(tag + "xdbi", [128, 8 * 512], BF16)
    wxpT = k.sb(tag + "wxpT", [128, NDC * 96], BF16)
    wdtT = k.sb(tag + "wdtT", [128, NDC * 128], BF16)
    dstg = k.sb(tag + "dstg", [128, 8 * 512], BF16)

    @k.d("sync")
    def _(s):
        return [s.dma_start(out=wxpT[:, kc * 96:(kc + 1) * 96],
                            in_=wxpb[:, kc * 128:(kc + 1) * 128], transpose=True)
                for kc in range(NDC)]

    @k.d("sync")
    def _(s):
        return [s.dma_start(out=wdtT[:, iI * 128:(iI + 1) * 128],
                            in_=wdtb[iI * 128:(iI + 1) * 128, :],
                            transpose=True) for iI in range(NDC)]

    for tt in range(TL // 512):
        t0 = tt * 512

        for kh in range(2):
            @k.d("sync")
            def _(s, t0=t0, kh=kh):
                return [s.dma_start(
                    out=xdbi[:, :].rearrange("p (a c) -> p a c", a=8),
                    in_=xbs_dram[kh * 1024:(kh + 1) * 1024, t0:t0 + 512]
                    .rearrange("(a p) c -> p a c", p=128))]

            @k.c("tensor")
            def _(t, kh=kh):
                ins = None
                for kk in range(8):
                    kc = kh * 8 + kk
                    sub = xdbi[:, kk * 512:(kk + 1) * 512]
                    t.matmul(cm.pp[0][0:64, :], wxpT[:, kc * 96: kc * 96 + 64],
                             sub, start=(kc == 0), stop=(kc == NDC - 1))
                    t.matmul(cm.pp[1][0:16, :],
                             wxpT[:, kc * 96 + 64: kc * 96 + 80],
                             sub, start=(kc == 0), stop=(kc == NDC - 1))
                    ins = t.matmul(cm.pp[2][0:16, :],
                                   wxpT[:, kc * 96 + 80: kc * 96 + 96],
                                   sub, start=(kc == 0), stop=(kc == NDC - 1))
                return ins

        @k.c("vector")
        def _(v, t0=t0):
            v.tensor_copy(dtl[:, t0:t0 + 512], cm.pp[0][0:64, :])
            v.tensor_copy(Bf[:, t0:t0 + 512], cm.pp[1][0:16, :])
            return v.tensor_copy(Cf[:, t0:t0 + 512], cm.pp[2][0:16, :])

        for iI in range(NDC):
            @k.c("tensor")
            def _(t, iI=iI, t0=t0):
                return t.matmul(cm.pp[3][:, :],
                                wdtT[0:64, iI * 128:(iI + 1) * 128],
                                dtl[:, t0:t0 + 512], start=True, stop=True)

            @k.c("vector")
            def _(v, iI=iI):
                return v.tensor_scalar(fsc[:, 0:512], cm.pp[3][:, :],
                                       bdtc[:, iI:iI + 1], None, ALU.add)

            @k.c("scalar")
            def _(sc):
                sc.activation(out=cm.sq[:, 512:1024], in_=fsc[:, 0:512],
                              func=AF.Abs)
                sc.activation(out=cm.sq[:, 512:1024], in_=cm.sq[:, 512:1024],
                              func=AF.Exp, scale=-1.0)
                return sc.activation(out=cm.sq[:, 512:1024],
                                     in_=cm.sq[:, 512:1024], func=AF.Ln,
                                     bias=cm.constcol(1.0)[0:1, :]
                                     if False else k.constcol(1.0)[:, 0:1])

            @k.c("vector")
            def _(v, iI=iI):
                return v.scalar_tensor_tensor(
                    dstg[:, (iI % 8) * 512:(iI % 8 + 1) * 512],
                    fsc[:, 0:512], 0.0, cm.sq[:, 512:1024],
                    ALU.max, ALU.add)

            if iI % 8 == 7:
                @k.d("sync")
                def _(s, t0=t0, iI=iI):
                    i0 = (iI // 8) * 8
                    return [s.dma_start(
                        out=dt_dram[i0 * 128:(i0 + 8) * 128, t0:t0 + 512]
                        .rearrange("(a p) c -> p a c", p=128),
                        in_=dstg[:, :].rearrange("p (a c) -> p a c", a=8))]
    k.pop_scope()
    TH = TL // 2

    def scan_pass(full):
        k.push_scope()
        sfx = "f" if full else "p"
        dt_c = k.sb(f"{tag}dtc{sfx}", [128, TH], BF16)
        dtx = k.sb(f"{tag}dtx{sfx}", [128, TH], BF16)
        abuf = k.sb(f"{tag}ab{sfx}", [128, TH], BF16)
        ubuf = k.sb(f"{tag}ub{sfx}", [128, TH], BF16)
        hbuf = k.sb(f"{tag}hb{sfx}", [128, TH], BF16)
        hsv = k.sb(f"{tag}hsv{sfx}", [128, 16], F32)
        ydc = k.sb(f"{tag}ydc{sfx}", [128, TH], F32) if full else None
        het = k.sb(f"{tag}het{sfx}", [128, 16], F32) if not full else None

        for dc in range(NDC):
            for th in range(2):
                t0h = th * TH

                @k.d("sync")
                def _(s, dc=dc, t0h=t0h):
                    return [s.dma_start(
                        out=dtx[:, :],
                        in_=xbs_dram[dc * 128:(dc + 1) * 128, t0h:t0h + TH]),
                        s.dma_start(
                        out=dt_c[:, :],
                        in_=dt_dram[dc * 128:(dc + 1) * 128, t0h:t0h + TH])]

                @k.c("vector")
                def _(v, dc=dc):
                    if full:
                        v.tensor_scalar(ydc[:, :], dtx[:, :],
                                        dcl[:, dc:dc + 1], None, ALU.mult)
                    return v.tensor_tensor(dtx[:, :], dt_c[:, :], dtx[:, :],
                                           ALU.mult)

                for n in range(16):
                    @k.c("scalar")
                    def _(sc, dc=dc, n=n):
                        return sc.activation(
                            out=abuf[:, :], in_=dt_c[:, :], func=AF.Exp,
                            scale=acl[:, dc * 16 + n:dc * 16 + n + 1])

                    @k.c("tensor")
                    def _(t, n=n, t0h=t0h):
                        ins = None
                        for j in range(4):
                            ins = t.matmul(
                                cm.pp[j][:, :],
                                selB[0:16, n * 128:(n + 1) * 128],
                                Bf[0:16, t0h + j * 512: t0h + (j + 1) * 512],
                                start=True, stop=True)
                        return ins

                    @k.c("vector")
                    def _(v):
                        ins = None
                        for j in range(4):
                            ins = v.tensor_tensor(
                                ubuf[:, j * 512:(j + 1) * 512],
                                dtx[:, j * 512:(j + 1) * 512],
                                cm.pp[j][:, :], ALU.mult)
                        return ins

                    if full and th == 0:
                        @k.c("vector")
                        def _(v, dc=dc, n=n):
                            return v.scalar_tensor_tensor(
                                ubuf[:, 0:1], abuf[:, 0:1],
                                hin[:, dc * 16 + n:dc * 16 + n + 1],
                                ubuf[:, 0:1], ALU.mult, ALU.add)

                    @k.c("vector")
                    def _(v, n=n, th=th):
                        init = 0.0 if th == 0 else hsv[:, n:n + 1]
                        return v.tensor_tensor_scan(hbuf[:, :], abuf[:, :],
                                                    ubuf[:, :], init,
                                                    ALU.mult, ALU.add)

                    @k.c("vector")
                    def _(v, n=n):
                        return v.tensor_copy(hsv[:, n:n + 1],
                                             hbuf[:, TH - 1:TH])

                    if full:
                        @k.c("tensor")
                        def _(t, n=n, t0h=t0h):
                            ins = None
                            for j in range(4):
                                ins = t.matmul(
                                    cm.pp[j][:, :],
                                    selB[0:16, n * 128:(n + 1) * 128],
                                    Cf[0:16, t0h + j * 512: t0h + (j + 1) * 512],
                                    start=True, stop=True)
                            return ins

                        @k.c("vector")
                        def _(v):
                            ins = None
                            for j in range(4):
                                sl = slice(j * 512, (j + 1) * 512)
                                v.tensor_tensor(fsc[:, 0:512], hbuf[:, sl],
                                                cm.pp[j][:, :], ALU.mult)
                                ins = v.tensor_add(ydc[:, sl], ydc[:, sl],
                                                   fsc[:, 0:512])
                            return ins
                if full:
                    @k.c("vector")
                    def _(v):
                        return v.tensor_copy(dt_c[:, :], ydc[:, :])

                    @k.d("sync")
                    def _(s, dc=dc, t0h=t0h):
                        return [s.dma_start(
                            out=y_dram[dc * 128:(dc + 1) * 128, t0h:t0h + TH],
                            in_=dt_c[:, :])]
            if not full:
                @k.c("vector")
                def _(v):
                    return v.tensor_copy(het[:, :], hsv[:, :])

                @k.d("sync")
                def _(s, dc=dc):
                    return [s.dma_start(out=hend_d[dc * 128:(dc + 1) * 128, :],
                                        in_=het[:, :])]
        k.pop_scope()

    scan_pass(False)

    def coll2(k2, blk, ph):
        total = ph.n

        def body(g):
            g.wait_ge(ph.h, total)
            g.collective_compute(
                "AllGather", ALU.bypass, replica_groups=PAIRS,
                ins=[hend_d.ap().opt()], outs=[hend_a.ap().opt()]).then_inc(ph.h, 1)
        blk.gpsimd(body)
        ph.n += 1
    k.manual(coll2)

    @k.d("sync")
    def _(s):
        return [s.dma_start(
            out=hin[:, :].rearrange("p (a n) -> p a n", a=NDC),
            in_=hend_a[0:NDI, :].rearrange("(a p) n -> p a n", p=128))]

    @k.c("vector")
    def _(v):
        return v.tensor_scalar(hin[:, :], hin[:, :], cm.flagcol[:, 0:1], None,
                               ALU.mult)

    scan_pass(True)

    k.push_scope()
    ysz = k.sb(tag + "ysz", [128, NDC * 512], BF16)
    ytmp = k.sb(tag + "ytmp", [128, 512], BF16)
    wos = k.sb(tag + "wos", [128, 1024], BF16)
    xnp = k.sb(tag + "xnp", [128, NC8 * 512], BF16)

    for tt in range(TL // 512):
        t0 = tt * 512
        cm.ln_stats_tile(t0)
        cm.ln_bcast()
        cm.ln_apply(t0, 512, xnp)

        @k.d("sync")
        def _(s, t0=t0):
            return [s.dma_start(
                out=ysz[:, :].rearrange("p (a c) -> p a c", a=NDC),
                in_=z_dram[:, t0:t0 + 512].rearrange("(a p) c -> p a c", p=128))]

        @k.c("scalar")
        def _(sc):
            return sc.activation(out=ysz[:, :], in_=ysz[:, :], func=AF.Silu)

        for kI in range(NDC):
            @k.d("sync")
            def _(s, t0=t0, kI=kI):
                return [s.dma_start(
                    out=ytmp[:, :],
                    in_=y_dram[kI * 128:(kI + 1) * 128, t0:t0 + 512])]

            @k.c("vector")
            def _(v, kI=kI):
                sl = ysz[:, kI * 512:(kI + 1) * 512]
                return v.tensor_tensor(sl, sl, ytmp[:, :], ALU.mult)

        for ocg in range(2):
            for kI in range(NDC):
                @k.d("sync")
                def _(s, kI=kI):
                    return [s.dma_start(out=wos[:, :],
                                        in_=woutb[:, kI * 128:(kI + 1) * 128],
                                        transpose=True)]

                @k.c("tensor")
                def _(t, ocg=ocg, kI=kI):
                    ins = None
                    for j in range(4):
                        oc = ocg * 4 + j
                        ins = t.matmul(cm.pp[j][:, :],
                                       wos[:, oc * 128:(oc + 1) * 128],
                                       ysz[:, kI * 512:(kI + 1) * 512],
                                       start=(kI == 0), stop=(kI == NDC - 1))
                    return ins

            @k.c("vector")
            def _(v, ocg=ocg, t0=t0):
                ins = None
                for j in range(4):
                    oc = ocg * 4 + j
                    sl = cm.strm[:, oc * TL + t0: oc * TL + t0 + 512]
                    v.scalar_tensor_tensor(sl, cm.pp[j][:, :],
                                           boc[:, oc:oc + 1], sl,
                                           ALU.add, ALU.add)
                    ins = v.tensor_add(sl, sl,
                                       xnp[:, oc * 512:(oc + 1) * 512])
                return ins
    k.pop_scope()



def emit_l0_attn(k, cm, W, tag):
    wqb = k.dram(tag + "_wqb", [D, D], BF16)
    wkb = k.dram(tag + "_wkb", [D, D], BF16)
    wvb = k.dram(tag + "_wvb", [D, D], BF16)
    wob = k.dram(tag + "_wob", [D, D], BF16)
    wsm = k.dram(tag + "_wsm", [48, D], BF16)
    xn_dram = k.dram(tag + "_xn", [D, TL], BF16)
    q_dram = k.dram(tag + "_q", [D, TL], BF16)
    kf_dram = k.dram(tag + "_kf", [D, TL], BF16)
    kt_dram = k.dram(tag + "_kt", [TL, D], BF16)
    vt_dram = k.dram(tag + "_vt", [TL, D], BF16)
    y_dram = k.dram(tag + "_y", [D, TL], BF16)
    S_dram = k.dram(tag + "_S", [128, 16 * 128], F32)
    z_dram = k.dram(tag + "_z", [128, 16], F32)
    S_all = k.dram(tag + "_Sa", [256, 16 * 128], F32)
    z_all = k.dram(tag + "_za", [256, 16], F32)

    Sf = k.sb(tag + "Sf", [128, 16 * 128], F32)
    Sb = k.sb(tag + "Sb", [128, 16 * 128], BF16)
    zf = k.sb(tag + "zf", [128, 16], F32)
    zb = k.sb(tag + "zb", [128, 16], BF16)
    wsmT = k.sb(tag + "wsmT", [128, NC8 * 48], BF16)
    bbg = k.sb(tag + "bbg", [128, 40], F32)
    lgrb = k.sb(tag + "lgrb", [128, 16], F32)
    ngm = k.sb(tag + "ngm", [128, 128], F32)
    identb = k.sb(tag + "identb", [128, 128], BF16)
    identf = k.sb(tag + "identf", [128, 128], F32)
    lng = k.sb(tag + "lng", [128, NC8], F32)
    lnb = k.sb(tag + "lnb", [128, NC8], F32)
    boc = k.sb(tag + "boc", [128, NC8], F32)
    sel16 = k.sb(tag + "sel16b", [16, 16 * 128], F32)
    brow = k.sb(tag + "brow0", [1, 1024], F32)
    lbf = k.sb(tag + "lbf", [16, 128], F32)
    lcf = k.sb(tag + "lcf", [16, 128], F32)
    zr16 = k.sb(tag + "zr16", [1, 32], F32)
    gtm = k.sb(tag + "gtm", [128, 8], F32)
    btm = k.sb(tag + "btm", [128, 16], F32)
    mtm = k.sb(tag + "mtm", [128, 16], F32)
    wmx = k.sb(tag + "wmx", [128, 16], BF16)
    lam = k.sb(tag + "lam", [128, 16], F32)
    dte = k.sb(tag + "dte", [128, 16], F32)
    gd = k.sb(tag + "gd", [128, 16], F32)
    den = k.sb(tag + "den", [128, 16], F32)
    mal = k.sb(tag + "mal", [128, 16], F32)
    pcb = k.sb(tag + "pcb", [128, 16], F32)
    lb1 = k.sb(tag + "lb1", [128, 128], F32)
    ga1 = k.sb(tag + "ga1", [128, 128], BF16)
    da1 = k.sb(tag + "da1", [128, 128], BF16)
    qs1 = k.sb(tag + "qs1", [128, 128], BF16)
    mw1 = k.sb(tag + "mw1", [128, 128], BF16)
    ych = k.sb(tag + "ych", [128, 8 * 128], F32)
    pci = k.sb(tag + "pci", [128, 128], BF16)
    fsc = cm.sq[:, 0:512]

    @k.d("gpsimd", wait=False)
    def _(g):
        return [g.dma_start(out=wqb[:, :], in_=W["Wq"]),
                g.dma_start(out=wkb[:, :], in_=W["Wk"]),
                g.dma_start(out=wvb[:, :], in_=W["Wv"]),
                g.dma_start(out=wob[:, :], in_=W["Wout"]),
                g.dma_start(out=wsm[0:16, :], in_=W["Wbeta"]),
                g.dma_start(out=wsm[16:32, :], in_=W["Wmix"]),
                g.dma_start(out=wsm[32:40, :], in_=W["Wgate"]),
                g.dma_start(out=wsm[40:48, :], in_=W["Wgate"])]

    @k.d("sync")
    def _(s):
        return [s.dma_start(out=wsmT[:, kc * 48:(kc + 1) * 48],
                            in_=wsm[:, kc * 128:(kc + 1) * 128], transpose=True)
                for kc in range(NC8)]

    @k.d("sync")
    def _(s):
        return [s.dma_start(out=brow[0:1, 0:16], in_=W["bbeta"][None, :]),
                s.dma_start(out=brow[0:1, 16:32], in_=W["bmix"][None, :]),
                s.dma_start(out=brow[0:1, 32:40], in_=W["bgate"][None, :]),
                s.dma_start(out=brow[0:1, 40:56], in_=W["lgr"]),
                s.dma_start(out=sel16[:, :], in_=W["sel16"]),
                s.dma_start(out=ngm[:, :], in_=W["ngm"])]

    @k.c("tensor")
    def _(t):
        t.matmul(cm.pp[0][:, 0:40], cm.ones[0:1, :], brow[0:1, 0:40],
                 start=True, stop=True)
        return t.matmul(cm.pp[1][:, 0:16], cm.ones[0:1, :], brow[0:1, 40:56],
                        start=True, stop=True)

    @k.c("vector")
    def _(v):
        v.tensor_copy(bbg[:, :], cm.pp[0][:, 0:40])
        return v.tensor_copy(lgrb[:, :], cm.pp[1][:, 0:16])

    @k.c("gpsimd")
    def _(g):
        g.affine_select(identf[:, :], cm.ones[:, :], [[1, 128]],
                        ALU.is_equal, 0.0, base=0, channel_multiplier=-1)
        return g.affine_select(identb[:, :], cm.ones[:, :], [[1, 128]],
                               ALU.is_equal, 0.0, base=0,
                               channel_multiplier=-1)

    def r2c(name_ap, dst, n):
        @k.d("sync")
        def _(s):
            return [s.dma_start(out=brow[0:1, 0:n * 128], in_=name_ap[None, :])]
        cm.vec_to_cols(brow[0:1, 0:n * 128], dst, n, cm.pp[0])

    r2c(W["ln_g"], lng[:, :], NC8)
    r2c(W["ln_b"], lnb[:, :], NC8)
    r2c(W["bout"], boc[:, :], NC8)

    k.push_scope()
    xn = k.sb(tag + "s1xn", [128, NC8 * 512], BF16)
    wbuf = k.sb(tag + "s1wb", [128, NC8 * 512], BF16)
    fmb = k.sb(tag + "s1fm", [128, NC8 * 512], BF16)
    tstg = k.sb(tag + "s1ts", [128, 1024], BF16)

    for tt in range(TL // 512):
        t0 = tt * 512
        cm.ln_stats_tile(t0)
        cm.ln_bcast()
        cm.ln_apply(t0, 512, xn)

        @k.d("sync")
        def _(s, t0=t0):
            return [s.dma_start(out=xn_dram[c * 128:(c + 1) * 128, t0:t0 + 512],
                                in_=xn[:, c * 512:(c + 1) * 512])
                    for c in range(NC8)]

        for wsrc, dst, elu, also_tm in [(wqb, q_dram, True, None),
                                        (wkb, kf_dram, True, kt_dram),
                                        (wvb, None, False, vt_dram)]:
            for half in range(2):
                @k.d("sync")
                def _(s, wsrc=wsrc, half=half):
                    return [s.dma_start(
                        out=wbuf[:, kc * 512:(kc + 1) * 512],
                        in_=wsrc[half * 512:(half + 1) * 512,
                                 kc * 128:(kc + 1) * 128],
                        transpose=True) for kc in range(NC8)]

                def elu_into(dst_ap, src_pp):
                    @k.c("vector")
                    def _(v):
                        return v.tensor_scalar_min(fsc[:, 0:512], src_pp, 0.0)

                    @k.c("scalar")
                    def _(sc):
                        return sc.activation(out=cm.sq[:, 512:1024],
                                             in_=fsc[:, 0:512], func=AF.Exp)

                    @k.c("vector")
                    def _(v):
                        v.scalar_tensor_tensor(fsc[:, 0:512], src_pp, 0.0,
                                               cm.sq[:, 512:1024],
                                               ALU.max, ALU.add)
                        return v.tensor_copy(dst_ap, fsc[:, 0:512])

                if dst is not None:
                    for oc4 in range(4):
                        oc = half * 4 + oc4

                        @k.c("tensor")
                        def _(t, oc4=oc4):
                            ins = None
                            for kc in range(NC8):
                                ins = t.matmul(
                                    cm.pp[oc4 % 2][:, :],
                                    wbuf[:, kc * 512 + oc4 * 128:
                                         kc * 512 + (oc4 + 1) * 128],
                                    xn[:, kc * 512:(kc + 1) * 512],
                                    start=(kc == 0), stop=(kc == NC8 - 1))
                            return ins
                        if elu:
                            elu_into(fmb[:, oc * 512:(oc + 1) * 512],
                                     cm.pp[oc4 % 2][:, :])
                        else:
                            @k.c("vector")
                            def _(v, oc=oc, oc4=oc4):
                                return v.tensor_copy(
                                    fmb[:, oc * 512:(oc + 1) * 512],
                                    cm.pp[oc4 % 2][:, :])

                if also_tm is not None:
                    for st in range(4):
                        @k.c("tensor")
                        def _(t, st=st):
                            ins = None
                            for kc in range(NC8):
                                ins = t.matmul(
                                    cm.pp[2][:, :],
                                    xn[:, kc * 512 + st * 128:
                                       kc * 512 + (st + 1) * 128],
                                    wbuf[:, kc * 512:(kc + 1) * 512],
                                    start=(kc == 0), stop=(kc == NC8 - 1))
                            return ins
                        if elu:
                            elu_into(tstg[:, 0:512], cm.pp[2][:, :])
                        else:
                            @k.c("vector")
                            def _(v, st=st):
                                return v.tensor_copy(tstg[:, 0:512],
                                                     cm.pp[2][:, :])

                        @k.d("sync")
                        def _(s, also_tm=also_tm, t0=t0, st=st, half=half):
                            return [s.dma_start(
                                out=also_tm[t0 + st * 128: t0 + (st + 1) * 128,
                                            half * 512:(half + 1) * 512],
                                in_=tstg[:, 0:512])]

            if dst is not None:
                @k.d("sync")
                def _(s, dst=dst, t0=t0):
                    return [s.dma_start(
                        out=dst[c * 128:(c + 1) * 128, t0:t0 + 512],
                        in_=fmb[:, c * 512:(c + 1) * 512])
                        for c in range(NC8)]
    k.pop_scope()

    def chunk_sweep(full):
        k.push_scope()
        sfx = "f" if full else "p"
        xnc = k.sb(f"{tag}cx{sfx}", [128, NC8 * 128], BF16)
        qc = k.sb(f"{tag}cq{sfx}", [128, NC8 * 128], BF16)
        kcf = k.sb(f"{tag}ck{sfx}", [128, NC8 * 128], BF16)
        ktc = k.sb(f"{tag}ckt{sfx}", [128, D], BF16)
        vtc = k.sb(f"{tag}cvt{sfx}", [128, D], BF16)
        ystg = k.sb(f"{tag}cys{sfx}", [128, D], BF16)

        for cc in range(TL // 128):
            cb = cc * 128

            @k.d("sync")
            def _(s, cb=cb):
                ins = [s.dma_start(
                    out=xnc[:, :].rearrange("p (a c) -> p a c", a=NC8),
                    in_=xn_dram[:, cb:cb + 128].rearrange("(a p) c -> p a c",
                                                          p=128)),
                    s.dma_start(out=ktc[:, :], in_=kt_dram[cb:cb + 128, :]),
                    s.dma_start(out=vtc[:, :], in_=vt_dram[cb:cb + 128, :])]
                if full:
                    ins.append(s.dma_start(
                        out=qc[:, :].rearrange("p (a c) -> p a c", a=NC8),
                        in_=q_dram[:, cb:cb + 128].rearrange("(a p) c -> p a c",
                                                             p=128)))
                    ins.append(s.dma_start(
                        out=kcf[:, :].rearrange("p (a c) -> p a c", a=NC8),
                        in_=kf_dram[:, cb:cb + 128].rearrange(
                            "(a p) c -> p a c", p=128)))
                return ins

            @k.c("tensor")
            def _(t):
                ins = None
                for kc2 in range(NC8):
                    ins = t.matmul(cm.pp[2][:, 0:40],
                                   xnc[:, kc2 * 128:(kc2 + 1) * 128],
                                   wsmT[:, kc2 * 48: kc2 * 48 + 40],
                                   start=(kc2 == 0), stop=(kc2 == NC8 - 1))
                return ins

            @k.c("vector")
            def _(v):
                v.tensor_tensor(btm[:, :], cm.pp[2][:, 0:16], bbg[:, 0:16],
                                ALU.add)
                v.tensor_tensor(mtm[:, :], cm.pp[2][:, 16:32], bbg[:, 16:32],
                                ALU.add)
                return v.tensor_tensor(gtm[:, :], cm.pp[2][:, 32:40],
                                       bbg[:, 32:40], ALU.add)

            @k.c("scalar")
            def _(sc):
                return sc.activation(out=gtm[:, :], in_=gtm[:, :],
                                     func=AF.Sigmoid)

            @k.c("scalar")
            def _(sc):
                return sc.activation(out=den[:, :], in_=btm[:, :], func=AF.Abs)

            @k.c("scalar")
            def _(sc):
                return sc.activation(out=den[:, :], in_=den[:, :], func=AF.Exp,
                                     scale=-1.0)

            @k.c("scalar")
            def _(sc):
                return sc.activation(out=den[:, :], in_=den[:, :], func=AF.Ln,
                                     bias=k.constcol(1.0)[:, 0:1])

            @k.c("vector")
            def _(v):
                return v.tensor_scalar(mal[:, :], btm[:, :], -1.0, None,
                                       ALU.mult)

            @k.c("vector")
            def _(v):
                return v.scalar_tensor_tensor(mal[:, :], mal[:, :], 0.0,
                                              den[:, :], ALU.max, ALU.add)

            @k.c("vector")
            def _(v):
                return v.tensor_tensor(mal[:, :], lgrb[:, :], mal[:, :],
                                       ALU.subtract)

            @k.c("vector")
            def _(v):
                return v.tensor_scalar(mal[:, :], mal[:, :],
                                       float(np.log(0.9999)), None, ALU.min)

            if full:
                @k.c("vector")
                def _(v):
                    m3 = mtm.rearrange("p (h j) -> p h j", j=2)
                    return v.tensor_tensor(
                        den[:, 0:8],
                        m3[:, :, 0:1].rearrange("p h j -> p (h j)"),
                        m3[:, :, 1:2].rearrange("p h j -> p (h j)"),
                        ALU.subtract)

                @k.c("scalar")
                def _(sc):
                    return sc.activation(out=den[:, 0:8], in_=den[:, 0:8],
                                         func=AF.Sigmoid)

                @k.c("vector")
                def _(v):
                    w3 = wmx.rearrange("p (h j) -> p h j", j=2)
                    v.tensor_copy(w3[:, :, 0:1].rearrange("p h j -> p (h j)"),
                                  den[:, 0:8])
                    return v.tensor_scalar(
                        w3[:, :, 1:2].rearrange("p h j -> p (h j)"),
                        den[:, 0:8], -1.0, 1.0, ALU.mult, ALU.add)

            @k.c("tensor")
            def _(t):
                t.matmul(cm.pp[3][0:1, 0:16], cm.ones_col[:, 0:1], mal[:, :],
                         start=True, stop=True)
                return t.transpose(cm.pp[4][0:16, 0:128], mal[:, :],
                                   identf[:, :])

            @k.c("vector")
            def _(v):
                v.tensor_copy(zr16[0:1, 0:16], cm.pp[3][0:1, 0:16])
                return v.tensor_copy(lbf[:, :], cm.pp[4][0:16, 0:128])

            @k.c("vector")
            def _(v):
                return v.tensor_tensor_scan(lcf[:, :], cm.ones[0:16, 0:128],
                                            lbf[:, :], 0.0, ALU.mult, ALU.add)

            @k.c("scalar")
            def _(sc):
                return sc.activation(out=zr16[0:1, 16:32], in_=zr16[0:1, 0:16],
                                     func=AF.Exp)

            @k.c("tensor")
            def _(t):
                t.transpose(cm.pp[3][0:128, 0:16], lcf[0:16, 0:128],
                            identf[0:16, 0:16])
                t.matmul(cm.pp[4][:, 0:16], cm.ones[0:1, :], zr16[0:1, 0:16],
                         start=True, stop=True)
                return t.matmul(cm.pp[5][:, 0:16], cm.ones[0:1, :],
                                zr16[0:1, 16:32], start=True, stop=True)

            @k.c("vector")
            def _(v):
                v.tensor_copy(lam[:, :], cm.pp[3][0:128, 0:16])
                return v.tensor_copy(pcb[:, :], cm.pp[5][:, 0:16])

            @k.c("vector")
            def _(v):
                return v.tensor_tensor(dte[:, :], cm.pp[4][:, 0:16],
                                       lam[:, :], ALU.subtract)

            @k.c("scalar")
            def _(sc):
                return sc.activation(out=dte[:, :], in_=dte[:, :], func=AF.Exp)

            @k.c("vector")
            def _(v):
                g3 = gtm.rearrange("p (h o) -> p h o", o=1)
                d3 = dte.rearrange("p (h j) -> p h j", j=2)
                return v.tensor_tensor(gd.rearrange("p (h j) -> p h j", j=2),
                                       d3, bcast_blocks(g3, d3), ALU.mult)

            for h in range(NC8):
                if full:
                    @k.c("tensor")
                    def _(t, h=h):
                        return t.matmul(cm.pp[0][:, 0:128],
                                        kcf[:, h * 128:(h + 1) * 128],
                                        qc[:, h * 128:(h + 1) * 128],
                                        start=True, stop=True)

                for j in range(2):
                    hj = h * 2 + j
                    if full:
                        @k.c("tensor")
                        def _(t, hj=hj):
                            return t.matmul(cm.pp[1][:, 0:128],
                                            sel16[0:16, hj * 128:(hj + 1) * 128],
                                            lcf[0:16, :], start=True, stop=True)

                        @k.c("vector")
                        def _(v, hj=hj):
                            return v.tensor_scalar(lb1[:, :],
                                                   cm.pp[1][:, 0:128],
                                                   lam[:, hj:hj + 1], None,
                                                   ALU.subtract)

                        @k.c("vector")
                        def _(v):
                            return v.tensor_tensor(lb1[:, :], lb1[:, :],
                                                   ngm[:, :], ALU.add)

                        @k.c("scalar")
                        def _(sc):
                            sc.activation(out=da1[:, :], in_=lb1[:, :],
                                          func=AF.Exp)
                            return sc.activation(out=ga1[:, :],
                                                 in_=cm.pp[1][:, 0:128],
                                                 func=AF.Exp)

                        @k.c("vector")
                        def _(v, h=h, hj=hj):
                            v.scalar_tensor_tensor(mw1[:, :],
                                                   cm.pp[0][:, 0:128],
                                                   gtm[:, h:h + 1], da1[:, :],
                                                   ALU.mult, ALU.mult)
                            return v.tensor_tensor(
                                qs1[:, :], qc[:, h * 128:(h + 1) * 128],
                                ga1[:, :], ALU.mult)

                        @k.c("tensor")
                        def _(t, h=h, hj=hj, j=j):
                            t.matmul(cm.pp[2][:, hj:hj + 1], mw1[:, :],
                                     cm.ones_colb[:, 0:1], start=(j == 0),
                                     stop=False)
                            t.matmul(cm.pp[2][:, hj:hj + 1], qs1[:, :],
                                     zb[:, hj:hj + 1], start=False,
                                     stop=(j == 1))
                            t.matmul(cm.pp[3 + j][:, 0:128], mw1[:, :],
                                     vtc[:, h * 128:(h + 1) * 128],
                                     start=True, stop=False)
                            return t.matmul(cm.pp[3 + j][:, 0:128], qs1[:, :],
                                            Sb[:, hj * 128:(hj + 1) * 128],
                                            start=False, stop=True)

                    @k.c("vector")
                    def _(v, h=h, hj=hj):
                        v.tensor_scalar(qs1[:, :],
                                        ktc[:, h * 128:(h + 1) * 128],
                                        gd[:, hj:hj + 1], None, ALU.mult)
                        return v.tensor_scalar(pci[:, :], identb[:, :],
                                               pcb[:, hj:hj + 1], None,
                                               ALU.mult)

                    @k.c("tensor")
                    def _(t, h=h, hj=hj):
                        t.matmul(cm.pp[5][:, 0:128], qs1[:, :],
                                 vtc[:, h * 128:(h + 1) * 128],
                                 start=True, stop=False)
                        t.matmul(cm.pp[5][:, 0:128], pci[:, :],
                                 Sb[:, hj * 128:(hj + 1) * 128],
                                 start=False, stop=False)
                        t.matmul(cm.pp[5][:, 128:129], qs1[:, :],
                                 cm.ones_colb[:, 0:1], start=False, stop=False)
                        return t.matmul(cm.pp[5][:, 128:129], pci[:, :],
                                        zb[:, hj:hj + 1], start=False,
                                        stop=True)

                    @k.c("vector")
                    def _(v, hj=hj):
                        v.tensor_copy(Sf[:, hj * 128:(hj + 1) * 128],
                                      cm.pp[5][:, 0:128])
                        v.tensor_copy(Sb[:, hj * 128:(hj + 1) * 128],
                                      cm.pp[5][:, 0:128])
                        v.tensor_copy(zf[:, hj:hj + 1], cm.pp[5][:, 128:129])
                        return v.tensor_copy(zb[:, hj:hj + 1],
                                             cm.pp[5][:, 128:129])

                if full:
                    @k.c("vector")
                    def _(v, h=h):
                        sl = den[:, 2 * h:2 * h + 2]
                        return v.tensor_scalar(sl,
                                               cm.pp[2][:, 2 * h:2 * h + 2],
                                               1e-6, None, ALU.add)

                    @k.c("vector")
                    def _(v, h=h):
                        sl = den[:, 2 * h:2 * h + 2]
                        return v.reciprocal(sl, sl)

                    @k.c("vector")
                    def _(v, h=h):
                        sl = mal[:, 2 * h:2 * h + 2]
                        return v.tensor_tensor(sl, den[:, 2 * h:2 * h + 2],
                                               wmx[:, 2 * h:2 * h + 2],
                                               ALU.mult)

                    @k.c("vector")
                    def _(v, h=h):
                        return v.tensor_scalar(fsc[:, 0:128],
                                               cm.pp[3][:, 0:128],
                                               mal[:, 2 * h:2 * h + 1], None,
                                               ALU.mult)

                    @k.c("vector")
                    def _(v, h=h):
                        return v.scalar_tensor_tensor(
                            ych[:, h * 128:(h + 1) * 128], cm.pp[4][:, 0:128],
                            mal[:, 2 * h + 1:2 * h + 2], fsc[:, 0:128],
                            ALU.mult, ALU.add)

            if full:
                for h in range(NC8):
                    @k.c("tensor")
                    def _(t, h=h):
                        return t.transpose(cm.pp[3][:, 0:128],
                                           ych[:, h * 128:(h + 1) * 128],
                                           identf[:, :])

                    @k.c("vector")
                    def _(v, h=h):
                        return v.tensor_copy(ystg[:, h * 128:(h + 1) * 128],
                                             cm.pp[3][:, 0:128])

                @k.d("sync")
                def _(s, cb=cb):
                    return [s.dma_start(
                        out=y_dram[:, cb:cb + 128].rearrange(
                            "(a p) c -> p a c", p=128),
                        in_=ystg[:, :].rearrange("p (a c) -> p a c", a=NC8))]
        k.pop_scope()

    @k.c("vector", wait=False)
    def _(v):
        v.memset(Sf[:, :], 0.0)
        v.memset(Sb[:, :], 0.0)
        v.memset(zf[:, :], 0.0)
        return v.memset(zb[:, :], 0.0)

    chunk_sweep(False)

    @k.d("sync")
    def _(s):
        return [s.dma_start(out=S_dram[:, :], in_=Sf[:, :]),
                s.dma_start(out=z_dram[:, :], in_=zf[:, :])]

    def coll(k2, blk, ph):
        total = ph.n

        def body(g):
            g.wait_ge(ph.h, total)
            g.collective_compute(
                "AllGather", ALU.bypass, replica_groups=PAIRS,
                ins=[S_dram.ap().opt()], outs=[S_all.ap().opt()]).then_inc(ph.h, 1)
            g.collective_compute(
                "AllGather", ALU.bypass, replica_groups=PAIRS,
                ins=[z_dram.ap().opt()], outs=[z_all.ap().opt()]).then_inc(ph.h, 1)
        blk.gpsimd(body)
        ph.n += 2
    k.manual(coll)

    @k.d("sync")
    def _(s):
        return [s.dma_start(out=Sf[:, :], in_=S_all[0:128, :]),
                s.dma_start(out=zf[:, :], in_=z_all[0:128, :])]

    @k.c("vector")
    def _(v):
        v.tensor_scalar(Sf[:, :], Sf[:, :], cm.flagcol[:, 0:1], None, ALU.mult)
        v.tensor_scalar(zf[:, :], zf[:, :], cm.flagcol[:, 0:1], None, ALU.mult)
        v.tensor_copy(Sb[:, :], Sf[:, :])
        return v.tensor_copy(zb[:, :], zf[:, :])

    chunk_sweep(True)

    k.push_scope()
    woT = k.sb(tag + "owT", [128, NC8 * 1024], BF16)
    ysl = k.sb(tag + "oys", [128, NC8 * 128], BF16)
    xnh = k.sb(tag + "oxn", [128, NC8 * 128], BF16)
    hres = k.sb(tag + "ohr", [128, NC8 * 128], F32)

    @k.d("sync")
    def _(s):
        return [s.dma_start(out=woT[:, kc * 1024:(kc + 1) * 1024],
                            in_=wob[:, kc * 128:(kc + 1) * 128], transpose=True)
                for kc in range(NC8)]

    for hh in range(TL // 128):
        hb = hh * 128

        @k.d("sync")
        def _(s, hb=hb):
            return [s.dma_start(
                out=ysl[:, :].rearrange("p (a c) -> p a c", a=NC8),
                in_=y_dram[:, hb:hb + 128].rearrange("(a p) c -> p a c",
                                                     p=128)),
                s.dma_start(
                out=xnh[:, :].rearrange("p (a c) -> p a c", a=NC8),
                in_=xn_dram[:, hb:hb + 128].rearrange("(a p) c -> p a c",
                                                      p=128))]

        for oc in range(NC8):
            @k.c("tensor")
            def _(t, oc=oc):
                ins = None
                for kc in range(NC8):
                    ins = t.matmul(cm.pp[oc % 2][:, 0:128],
                                   woT[:, kc * 1024 + oc * 128:
                                       kc * 1024 + (oc + 1) * 128],
                                   ysl[:, kc * 128:(kc + 1) * 128],
                                   start=(kc == 0), stop=(kc == NC8 - 1))
                return ins

            @k.c("vector")
            def _(v, oc=oc):
                sl = hres[:, oc * 128:(oc + 1) * 128]
                return v.tensor_scalar(sl, cm.pp[oc % 2][:, 0:128],
                                       boc[:, oc:oc + 1], None, ALU.add)

            @k.c("vector")
            def _(v, oc=oc):
                sl = hres[:, oc * 128:(oc + 1) * 128]
                return v.tensor_add(sl, sl, xnh[:, oc * 128:(oc + 1) * 128])

        cm.ln_stats_tile(0, src=hres, src_stride=128, n=128)
        cm.ln_bcast(n=128)

        for oc in range(NC8):
            @k.c("vector")
            def _(v, oc=oc):
                return v.tensor_sub(fsc[:, 0:128],
                                    hres[:, oc * 128:(oc + 1) * 128],
                                    cm.pp[4][:, 0:128])

            @k.c("vector")
            def _(v):
                return v.tensor_mul(fsc[:, 0:128], fsc[:, 0:128],
                                    cm.pp[5][:, 0:128])

            @k.c("vector")
            def _(v, oc=oc):
                return v.tensor_scalar(fsc[:, 0:128], fsc[:, 0:128],
                                       lng[:, oc:oc + 1], None, ALU.mult)

            @k.c("vector")
            def _(v, oc=oc):
                return v.tensor_scalar(fsc[:, 0:128], fsc[:, 0:128],
                                       lnb[:, oc:oc + 1], None, ALU.add)

            @k.c("vector")
            def _(v, oc=oc, hb=hb):
                sl = cm.strm[:, oc * TL + hb: oc * TL + hb + 128]
                return v.tensor_add(sl, sl, fsc[:, 0:128])
    k.pop_scope()



def _host_consts():
    slopes = np.array([0.5 ** (i + 1) for i in range(8)], np.float32)
    lgr = np.repeat(-slopes, 2)[None, :].astype(np.float32)
    ngm = np.where(np.arange(128)[None, :] >= np.arange(128)[:, None],
                   0.0, -30000.0).astype(np.float32)
    sel16 = np.zeros((16, 16 * 128), np.float32)
    for hj in range(16):
        sel16[hj, hj * 128:(hj + 1) * 128] = 1.0
    selpair = np.zeros((16, 8 * 128), np.float32)
    mask16 = np.zeros((128, 8 * 16), np.float32)
    for c in range(8):
        for p in range(128):
            h = 2 * c + (1 if p >= 64 else 0)
            selpair[h, c * 128 + p] = 1.0
            mask16[p, c * 16 + h] = 1.0
    return {"lgr": lgr, "ngm": ngm, "sel16": sel16,
            "selpair": selpair, "mask16": mask16}


def _mk_launch(emit, wspecs, consts=()):
    k = K()
    sin = k.dram("strm_in", [D, TL], F32, kind="ExternalInput").ap()
    sout = k.dram("strm_out", [D, TL], F32, kind="ExternalOutput").ap()
    W = {}
    for nm, shape in wspecs:
        W[nm] = k.dram(nm, list(shape), F32, kind="ExternalInput").ap()
    for nm, shape in consts:
        W[nm] = k.dram(nm, list(shape), F32, kind="ExternalInput").ap()
    cm = Common(k)
    cm.load_consts()
    cm.load_stream(sin)
    emit(k, cm, W)
    cm.store_stream(sout)

    @k.c("vector")
    def _(v):
        return v.memset(cm.one1[:1, :], 1.0)
    return k.finalize()


_LAUNCH_DEFS = [
    ("L0a", lambda k, cm, W: emit_l0_attn(k, cm, W, "d0"),
     [("Wq", (D, D)), ("Wk", (D, D)), ("Wv", (D, D)), ("Wout", (D, D)),
      ("Wbeta", (16, D)), ("Wmix", (16, D)), ("Wgate", (8, D)),
      ("bbeta", (16,)), ("bmix", (16,)), ("bgate", (8,)), ("bout", (D,)),
      ("ln_g", (D,)), ("ln_b", (D,))],
     [("lgr", (1, 16)), ("ngm", (128, 128)), ("sel16", (16, 2048))],
     {"Wq": "L0_attn_Wq", "Wk": "L0_attn_Wk", "Wv": "L0_attn_Wv",
      "Wout": "L0_attn_Wout", "Wbeta": "L0_attn_Wbeta",
      "Wmix": "L0_attn_Wmix", "Wgate": "L0_attn_Wgate",
      "bbeta": "L0_attn_bbeta", "bmix": "L0_attn_bmix",
      "bgate": "L0_attn_bgate", "bout": "L0_attn_bout",
      "ln_g": "L0_attn_ln_g", "ln_b": "L0_attn_ln_b"}),
    ("L0f", lambda k, cm, W: emit_ffn_fp4(k, cm, W["Wg"], W["Wu"], W["Wd"],
                                          "f0"),
     [("Wg", (2816, D)), ("Wu", (2816, D)), ("Wd", (D, 2816))], [],
     {"Wg": "L0_ffn_w_gate", "Wu": "L0_ffn_w_up", "Wd": "L0_ffn_w_down"}),
    ("L1a", lambda k, cm, W: emit_l1_attn(k, cm, W, "m1"),
     [("norm_w", (D,)), ("Wx", (2048, D)), ("bx", (2048,)),
      ("Wz", (2048, D)), ("bz", (2048,)), ("conv_w", (2048, 1, 4)),
      ("conv_b", (2048,)), ("Wxp", (96, 2048)), ("Wdt", (2048, 64)),
      ("bdt", (2048,)), ("A_log", (2048, 16)), ("D", (2048,)),
      ("Wout", (D, 2048)), ("bout", (D,))], [],
     {"norm_w": "L1_attn_norm_w", "Wx": "L1_attn_Wx", "bx": "L1_attn_bx",
      "Wz": "L1_attn_Wz", "bz": "L1_attn_bz", "conv_w": "L1_attn_conv_w",
      "conv_b": "L1_attn_conv_b", "Wxp": "L1_attn_Wxp",
      "Wdt": "L1_attn_Wdt", "bdt": "L1_attn_bdt", "A_log": "L1_attn_A_log",
      "D": "L1_attn_D", "Wout": "L1_attn_Wout", "bout": "L1_attn_bout"}),
    ("L1f", lambda k, cm, W: emit_l1_ffn(k, cm, W, "lf"),
     [("w_r", (D, 1)), ("b_r", (1,)), ("W1", (D, 4096)), ("b1", (4096,)),
      ("W2", (4096, D)), ("b2", (D,))], [],
     {"w_r": "L1_ffn_w_r", "b_r": "L1_ffn_b_r", "W1": "L1_ffn_W1",
      "b1": "L1_ffn_b1", "W2": "L1_ffn_W2", "b2": "L1_ffn_b2"}),
    ("L2a", lambda k, cm, W: emit_l2_attn(k, cm, W, "l2"),
     [("wg_w", (D, D)), ("wg_b", (D,)), ("wq_w", (D, D)), ("wq_b", (D,)),
      ("wk_w", (D, D)), ("wk_b", (D,)), ("wv_w", (D, D)), ("wv_b", (D,)),
      ("out_w", (D, D)), ("out_b", (D,))],
     [("selpair", (16, 1024)), ("mask16", (128, 128))],
     {"wg_w": "L2_attn_wg_w", "wg_b": "L2_attn_wg_b",
      "wq_w": "L2_attn_wq_w", "wq_b": "L2_attn_wq_b",
      "wk_w": "L2_attn_wk_w", "wk_b": "L2_attn_wk_b",
      "wv_w": "L2_attn_wv_w", "wv_b": "L2_attn_wv_b",
      "out_w": "L2_attn_out_w", "out_b": "L2_attn_out_b"}),
    ("L2f", lambda k, cm, W: emit_l2_ffn(k, cm, W, "mo"),
     [("router_w", (D, 8)), ("router_b", (8,)), ("w1", (8, D, 8192)),
      ("w2", (8, 4096, D))], [],
     {"router_w": "L2_ffn_router_w", "router_b": "L2_ffn_router_b",
      "w1": "L2_ffn_w1", "w2": "L2_ffn_w2"}),
    ("L3a", lambda k, cm, W: emit_l3_attn(k, cm, W, "l3"),
     [("qkv_w", (192, D)), ("qkv_b", (192,)), ("proj_w", (D, D)),
      ("proj_b", (D,))], [],
     {"qkv_w": "L3_attn_qkv_w", "qkv_b": "L3_attn_qkv_b",
      "proj_w": "L3_attn_proj_w", "proj_b": "L3_attn_proj_b"}),
    ("L3f", lambda k, cm, W: emit_ffn_fp4(k, cm, W["Wg"], W["Wu"], W["Wd"],
                                          "f3"),
     [("Wg", (2816, D)), ("Wu", (2816, D)), ("Wd", (D, 2816))], [],
     {"Wg": "L3_ffn_w_gate", "Wu": "L3_ffn_w_up", "Wd": "L3_ffn_w_down"}),
]

_PROGRAMS = {}


def _get_program(name):
    if name not in _PROGRAMS:
        for nm, emit, wspecs, consts, imap in _LAUNCH_DEFS:
            if nm == name:
                _PROGRAMS[name] = (_mk_launch(emit, wspecs, consts), wspecs,
                                   consts, imap)
                break
    return _PROGRAMS[name]


def kernel(**inputs):
    hc = _host_consts()
    x = np.asarray(inputs["L0_attn_x"], dtype=np.float32)
    streams = []
    for c in range(8):
        b, half = c // 2, c % 2
        streams.append(np.ascontiguousarray(
            x[b, half * TL:(half + 1) * TL, :].T))

    for name, emit, wspecs, consts, imap in _LAUNCH_DEFS:
        nc, _, _, _ = _get_program(name)
        ims = []
        for c in range(8):
            im = {"strm_in": streams[c],
                  "flag_in": np.full((1, 1), float(c % 2), np.float32)}
            for nm, shape in wspecs:
                im[nm] = np.ascontiguousarray(
                    np.asarray(inputs[imap[nm]], dtype=np.float32)
                    .reshape(shape))
            for nm, shape in consts:
                im[nm] = hc[nm].reshape(shape)
            ims.append(im)
        res = run_bass_kernel_spmd(nc, ims, core_ids=list(range(8)))
        streams = [np.asarray(res.results[c]["strm_out"]) for c in range(8)]

    out = np.empty((B, T, D), np.float32)
    for c in range(8):
        b, half = c // 2, c % 2
        out[b, half * TL:(half + 1) * TL, :] = streams[c].T
    return out
